# version 15
# baseline (speedup 1.0000x reference)
import contextlib
import os
import numpy as np
import ml_dtypes
import concourse.bass as bass
import concourse.mybir as mybir
from concourse.bass_utils import run_bass_kernel_spmd

F32 = mybir.dt.float32
BF16 = mybir.dt.bfloat16
U8 = mybir.dt.uint8
AF = mybir.ActivationFunctionType
ALU = mybir.AluOpType
BFD = ml_dtypes.bfloat16

H = 4096
SEQ = 2048
TOWN = 1024
NH = 32
DH = 128
II = 11008
NB = 43
HC = 32
NCORES = 8
ROUND_BLOCKS = [4] * 10 + [3]
NRND = len(ROUND_BLOCKS)
MASK_NEG = -1.0e9
EPS = 1e-5
KBISECT = int(os.environ.get('KBISECT', '5'))
KB_ROUNDS = int(os.environ.get('KB_ROUNDS', '99'))
KB_DBG = int(os.environ.get('KB_DBG', '0'))


def _slopes():
    start = 2.0 ** (-(2.0 ** (-(np.log2(NH) - 3))))
    return np.array([start ** (i + 1) for i in range(NH)], dtype=np.float32)


def _plane_perm():
    return np.concatenate([np.arange(0, H, 2), np.arange(1, H, 2)])


def _ic_order():
    cols = []
    for b in range(NB):
        cols.append(np.arange(256 * b, 256 * b + 256, 2))
        cols.append(np.arange(256 * b + 1, 256 * b + 256, 2))
    return np.concatenate(cols)


def _u8_of_i32(a):
    return np.ascontiguousarray(
        a.view(np.uint8).reshape(a.shape[0], a.shape[1], 4)[:, :, 0])


OFFS = {}
ENGINES = ("tensor", "vector", "scalar", "gpsimd", "sync")


class Prog:
    def __init__(self):
        self.ops = {e: [] for e in ENGINES}
        self.plan = {}
        self.ensured = {}

    def note(self, s):
        if s not in self.plan:
            self.plan[s] = 0

    def cur(self, s):
        self.note(s)
        return self.plan[s]

    def op(self, eng, fn, waits=(), inc=None):
        rw = []
        for (s, v) in waits:
            self.note(s)
            assert v <= self.plan[s], f"wait on future {s}: {v} > {self.plan[s]}"
            if v <= 0:
                continue
            k = (eng, s)
            if self.ensured.get(k, 0) >= v:
                continue
            self.ensured[k] = v
            rw.append((s, v))
        if inc is not None:
            s, amt = inc
            self.note(s)
            self.plan[s] += amt
        self.ops[eng].append((rw, fn, inc))

    def barrier(self, name):
        for e in ENGINES:
            self.op(e, lambda en: en.nop(), inc=(name, 1))
        tgt = self.cur(name)
        for e in ENGINES:
            self.op(e, lambda en: en.nop(), waits=[(name, tgt)])

    def emit(self, eng, e, sems):
        for (waits, fn, inc) in self.ops[eng]:
            for (s, v) in waits:
                e.wait_ge(sems[s], v)
            ins = fn(e)
            if inc is not None:
                ins.then_inc(sems[inc[0]], inc[1])


class Arena:
    def __init__(self, ap, total, name):
        self.ap = ap
        self.total = total
        self.off = 0
        self.marks = []
        self.name = name
        self.peak = 0

    def alloc(self, n, align=128, tag=None):
        o = (self.off + align - 1) // align * align
        assert o + n <= self.total, f"{self.name} overflow: {o}+{n} > {self.total}"
        self.off = o + n
        self.peak = max(self.peak, self.off)
        if tag is not None:
            OFFS[tag] = (self.name, o, n)
        return self.ap[:, o:o + n]

    def push(self):
        self.marks.append(self.off)

    def pop(self):
        self.off = self.marks.pop()


class Ctr:
    def __init__(self, prog, name):
        self.p, self.name = prog, name

    def inc(self, amt=1):
        return (self.name, amt)

    def w(self):
        return [(self.name, self.p.cur(self.name))]


class Bank:
    def __init__(self, name, ap):
        self.name, self.ap = name, ap
        self.w = 0
        self.r = 0

    def ww(self):
        return [] if self.r == 0 else [(f"{self.name}R", self.r)]

    def wi(self):
        self.w += 1
        return (f"{self.name}W", 1)

    def rw(self):
        return [(f"{self.name}W", self.w)]

    def ri(self):
        self.r += 1
        return (f"{self.name}R", 1)

    def rdone(self):
        return [] if self.r == 0 else [(f"{self.name}R", self.r)]

    def wdone(self):
        return [] if self.w == 0 else [(f"{self.name}W", self.w)]


def host_prep(inp):
    pp = _plane_perm()
    ico = _ic_order()
    x = np.asarray(inp["L0_attn_hidden_states"], dtype=np.float32)
    Wq = np.asarray(inp["L0_attn_Wq"], dtype=np.float32)
    Wkv = np.asarray(inp["L0_attn_Wkv"], dtype=np.float32)
    Wo = np.asarray(inp["L0_attn_Wo"], dtype=np.float32)

    wq_t = np.ascontiguousarray(Wq[pp, :].reshape(HC, 128, NH, DH).transpose(2, 1, 0, 3))
    wkv_t = np.ascontiguousarray(Wkv[pp, :].reshape(HC, 128, 256).transpose(1, 0, 2))
    wo_t = np.ascontiguousarray(Wo.reshape(HC, 128, 8, 512).transpose(2, 1, 0, 3))

    def prep_gu(wname):
        u8 = _u8_of_i32(np.asarray(inp[wname]))
        u8r = np.ascontiguousarray(u8.T)[:, ico]
        return np.ascontiguousarray(u8r.reshape(16, 128, NB, 256).transpose(2, 1, 0, 3))

    g8 = prep_gu("L0_ffn_gate_Wq")
    up8 = prep_gu("L0_ffn_up_Wq")
    d8 = _u8_of_i32(np.asarray(inp["L0_ffn_down_Wq"]))
    d8t = np.ascontiguousarray(d8.T.reshape(NB, 128, H))

    sT_g = np.ascontiguousarray(np.asarray(inp["L0_ffn_gate_scale"], np.float32).T[:, ico])
    zT_g = np.ascontiguousarray(np.asarray(inp["L0_ffn_gate_zero"], np.float32).T[:, ico])
    sT_u = np.ascontiguousarray(np.asarray(inp["L0_ffn_up_scale"], np.float32).T[:, ico])
    zT_u = np.ascontiguousarray(np.asarray(inp["L0_ffn_up_zero"], np.float32).T[:, ico])
    sT_d = np.ascontiguousarray(np.asarray(inp["L0_ffn_down_scale"], np.float32).T)
    zT_d = np.ascontiguousarray(np.asarray(inp["L0_ffn_down_zero"], np.float32).T)

    B1 = np.zeros((H, 64), dtype=BFD)
    B1[np.arange(H), pp // 64] = -1.0
    b1_t = np.ascontiguousarray(B1.reshape(HC, 128, 64).transpose(1, 0, 2))

    Bb = np.zeros((128, 60), dtype=BFD)
    for pr in range(128):
        Bb[pr, 28 + pr // 32] = -1.0

    shared = dict(wq=wq_t, wkv=wkv_t, wo=wo_t, g8=g8, up8=up8, d8=d8t,
                  sT_g=sT_g, zT_g=zT_g, sT_u=sT_u, zT_u=zT_u, sT_d=sT_d,
                  zT_d=zT_d, b1=b1_t, bbuf=Bb)

    own_blocks = [[(0, 512), (1536, 2048)], [(512, 1024), (1024, 1536)]]
    maps = []
    for core in range(NCORES):
        seq = core // 2
        blocks = own_blocks[core % 2]
        own_idx = np.concatenate([np.arange(a, b) for (a, b) in blocks])
        xT = np.ascontiguousarray(x[seq].T[pp, :])
        xTown = np.ascontiguousarray(xT[:, own_idx])
        xown = np.ascontiguousarray(x[seq][own_idx, :])
        m2 = []
        for (a, _b) in blocks:
            c = np.arange(2432)
            pr = np.arange(128)
            dd = a + c[None, :] - 1920 - pr[:, None]
            m2.append(np.where(dd >= 0, -dd, MASK_NEG).astype(np.float32))
        m = dict(shared)
        m.update(xT=xT, xTown=xTown, xown=xown, m2q0=m2[0], m2q1=m2[1])
        maps.append(m)
    return maps, own_blocks


def build_program():
    slopesf = [float(s) for s in _slopes()]
    sd = float(1.0 / np.sqrt(DH))
    nc = bass.Bass("TRN2", target_bir_lowering=False)

    d = {}
    for name, shape, dt in [
        ("xT", [H, SEQ], F32), ("xTown", [H, TOWN], F32), ("xown", [TOWN, H], F32),
        ("m2q0", [128, 2432], F32), ("m2q1", [128, 2432], F32),
        ("wq", [NH, 128, HC, DH], F32), ("wkv", [128, HC, 256], F32),
        ("wo", [8, 128, HC, 512], F32),
        ("g8", [NB, 128, 16, 256], U8), ("up8", [NB, 128, 16, 256], U8),
        ("d8", [NB, 128, H], U8),
        ("sT_g", [64, II], F32), ("zT_g", [64, II], F32),
        ("sT_u", [64, II], F32), ("zT_u", [64, II], F32),
        ("sT_d", [172, H], F32), ("zT_d", [172, H], F32),
        ("b1", [128, HC, 64], BF16), ("bbuf", [128, 60], BF16),
    ]:
        d[name] = nc.dram_tensor(name, shape, dt, kind="ExternalInput").ap()
    out_d = nc.dram_tensor("out", [TOWN, H], F32, kind="ExternalOutput").ap()
    kdbg = "ExternalOutput" if KB_DBG else "Internal"
    dpart = nc.dram_tensor("dpart", [TOWN, H], F32, kind=kdbg).ap()
    gbounce = nc.dram_tensor("gbounce", [TOWN, H], BF16, kind=kdbg).ap()
    kv_dram = nc.dram_tensor("kv_dram", [128, 2 * SEQ], BF16, kind=kdbg).ap()
    stg_g = nc.dram_tensor("stg_g", [64, II], BF16, kind="Internal").ap()
    stg_u = nc.dram_tensor("stg_u", [64, II], BF16, kind="Internal").ap()
    stg_d = nc.dram_tensor("stg_d", [172, H], BF16, kind="Internal").ap()
    stg_g2 = nc.dram_tensor("stg_g2", [4, NB, 16, 256], BF16, kind="Internal").ap()
    stg_u2 = nc.dram_tensor("stg_u2", [4, NB, 16, 256], BF16, kind="Internal").ap()
    stg_d2 = nc.dram_tensor("stg_d2", [4, 8, NB, 512], BF16, kind="Internal").ap()
    if KB_DBG:
        dbg_xn = nc.dram_tensor("dbg_xn", [128, HC * TOWN], BF16, kind="ExternalOutput").ap()
        dbg_av = nc.dram_tensor("dbg_av", [128, NH * TOWN], BF16, kind="ExternalOutput").ap()

    p = Prog()

    BF_COLS = 84992
    F32_COLS = 7424
    U8_COLS = 12288

    with (
        nc.sbuf_tensor("a_bf", [128, BF_COLS], BF16) as a_bf_t,
        nc.sbuf_tensor("a_f32", [128, F32_COLS], F32) as a_f32_t,
        nc.sbuf_tensor("a_u8", [128, U8_COLS], U8) as a_u8_t,
        nc.sbuf_tensor("ones1", [1, 128], BF16) as ones1_t,
        nc.sbuf_tensor("onescol", [128, 1], BF16) as onescol_t,
        nc.sbuf_tensor("epsc", [128, 1], F32) as epsc_t,
    ):
        abf = Arena(a_bf_t[:], BF_COLS, "bf16")
        af32 = Arena(a_f32_t[:], F32_COLS, "f32")
        au8 = Arena(a_u8_t[:], U8_COLS, "u8")
        ones1 = ones1_t[:]
        onescol = onescol_t[:]
        epsc = epsc_t[:]

        pstack = contextlib.ExitStack()
        banks = []
        for i in range(8):
            t = pstack.enter_context(nc.psum_tensor(f"pb{i}", [128, 512], F32))
            banks.append(Bank(f"b{i}", t[:]))

        def dma(eng, out, in_, inc, waits=()):
            p.op(eng, lambda e, o=out, i=in_: e.dma_start(out=o, in_=i),
                 waits=waits, inc=inc)

        def mmop(bank_ap, lhsT, rhs, start, stop, waits=(), inc=None):
            p.op("tensor", lambda e, o=bank_ap, l=lhsT, r=rhs, s1=start, s2=stop:
                 e.matmul(o, l, r, start=s1, stop=s2), waits=waits, inc=inc)

        def act(out, in_, func, bias=None, scale=None, waits=(), inc=None):
            kw = {}
            if bias is not None:
                kw["bias"] = bias
            if scale is not None:
                kw["scale"] = scale
            p.op("scalar", lambda e, o=out, i=in_, f=func, k=dict(kw):
                 e.activation(o, i, f, **k), waits=waits, inc=inc)

        def vop(eng, fn, waits=(), inc=None):
            p.op(eng, fn, waits=waits, inc=inc)

        def C(name):
            return Ctr(p, name)

        bbuf_s = abf.alloc(64)
        xn_own = abf.alloc(HC * TOWN, tag="xn_own")
        xgT = abf.alloc(TOWN, tag="xgT")

        dma("sync", bbuf_s[:, 0:60], d["bbuf"], ("c_bb", 16))
        abf.push()
        stgb = abf.alloc(II)
        stgc = C("stgc")
        stgo = C("stgo")
        stgo_h2w = []
        for snm, ddr, dst, rows in (("sg", d["sT_g"], stg_g, 64),
                                    ("su", d["sT_u"], stg_u, 64),
                                    ("sd", d["sT_d"], stg_d, 172)):
            if rows == 64:
                dma("gpsimd", stgb[0:64, 0:II], ddr, stgc.inc(16), waits=stgo.w())
                dma("sync", dst, stgb[0:64, 0:II], stgo.inc(16), waits=stgc.w())
            else:
                dma("gpsimd", stgb[0:128, 0:H], ddr[0:128, :], stgc.inc(16),
                    waits=stgo.w())
                dma("gpsimd", stgb[0:44, H:2 * H], ddr[128:172, :], stgc.inc(16))
                dma("sync", dst[0:128, :], stgb[0:128, 0:H], stgo.inc(16),
                    waits=stgc.w())
                dma("sync", dst[128:172, :], stgb[0:44, H:2 * H], stgo.inc(16))
        h2t = abf.alloc(4096)
        h2c = C("h2c")
        for gp in range(4):
            for wsrc, wdst, pat in (
                    (stg_g, stg_g2, "gu"), (stg_u, stg_u2, "gu"), (stg_d, stg_d2, "d")):
                if pat == "gu":
                    src2 = wsrc.rearrange("(jc g) (blk i) -> g blk jc i", g=4, i=256)[gp]
                    dst2 = wdst[gp]
                else:
                    src2 = wsrc.rearrange("(blk g) (hg i) -> g blk hg i", g=4, i=512)[gp]
                    dst2 = wdst[gp].rearrange("hg blk i -> blk hg i")
                dma("gpsimd", h2t[0:NB, :], src2, h2c.inc(16),
                    waits=stgo.w() + stgo_h2w)
                dma("sync", dst2, h2t[0:NB, :], stgo.inc(16), waits=h2c.w())
                stgo_h2w = stgo.w()
        stg_done = stgo.w()
        p.barrier("ph0")
        abf.pop()
        p.op("gpsimd", lambda e: e.memset(ones1, 1.0), inc=("c_o1", 1))
        p.op("gpsimd", lambda e, o=epsc: e.memset(o, EPS), inc=("c_ep", 1))
        p.op("gpsimd", lambda e: e.memset(onescol, 1.0), inc=("c_oc", 1))

        abf.push()
        af32.push()
        wkv_s = abf.alloc(HC * 256)
        b1_s = abf.alloc(HC * 64)
        kT = abf.alloc(SEQ, tag="kT")
        V = abf.alloc(SEQ, tag="V")
        xnq = abf.alloc(HC * 512, tag="xnq")
        sqb = [abf.alloc(512), abf.alloc(512)]
        mean_b = abf.alloc(512)
        rstd_b = abf.alloc(512)
        m_rep = abf.alloc(512)
        r_rep = abf.alloc(512)
        xf = [af32.alloc(512) for _ in range(3)]
        xf_free = [[], [], []]
        strow = af32.alloc(512)
        msq = af32.alloc(512)
        varr = af32.alloc(512)
        lnv = af32.alloc(512)
        rsr = af32.alloc(512)

        dma("gpsimd", wkv_s, d["wkv"], ("c_kv", 16), waits=stg_done)
        dma("sync", b1_s, d["b1"], ("c_b1", 16), waits=stg_done)

        lnx = C("lnx")
        sqc = C("sqc")
        lnr = C("lnr")
        lnmb = C("lnmb")
        lnrb = C("lnrb")
        nrm = C("nrm")

        def ln_pass(xdram, cs, xn_ap, xcol0, xstride, xw=()):
            bkx, bkq = banks[0], banks[1]
            prev_b1w = bkq.wdone()
            xbc_at = []
            for hc in range(HC):
                b3 = hc % 3
                dma("sync", xf[b3], xdram[hc * 128:(hc + 1) * 128, cs:cs + 512],
                    (f"lnx{b3}", 16), waits=list(xw) + xf_free[b3])
                xc = xn_ap[:, xcol0 + hc * xstride: xcol0 + hc * xstride + 512]
                act(xc, xf[b3], AF.Copy,
                    waits=[(f"lnx{b3}", p.cur(f"lnx{b3}"))] + prev_b1w + list(xw),
                    inc=("xbc", 1))
                xf_free[b3] = [("xbc", p.cur("xbc"))]
                xbc_at.append(p.cur("xbc"))
            for hc in range(HC):
                b = hc % 2
                xc = xn_ap[:, xcol0 + hc * xstride: xcol0 + hc * xstride + 512]
                sqfree = [(f"sqF{b}", p.cur(f"sqF{b}"))]
                act(sqb[b], xc, AF.Square,
                    waits=sqfree, inc=sqc.inc())
                mmop(bkx.ap[0:1, :], onescol, xc, hc == 0, hc == HC - 1,
                     waits=(bkx.ww() + [("c_oc", 1)] if hc == 0 else []) +
                     [("xbc", xbc_at[hc])],
                     inc=bkx.wi() if hc == HC - 1 else None)
                mi = (f"sqF{b}", 1) if hc != HC - 1 else bkq.wi()
                mmop(bkq.ap[0:1, :], onescol, sqb[b], hc == 0, hc == HC - 1,
                     waits=sqc.w() + (bkq.ww() if hc == 0 else []), inc=mi)
            vop("vector", lambda e, o=strow[0:1, :], i=bkx.ap[0:1, :]:
                e.tensor_scalar(o, i, 1.0 / H, None, ALU.mult),
                waits=bkx.rw(), inc=bkx.ri())
            vop("vector", lambda e, o=msq[0:1, :], i=strow[0:1, :]:
                e.tensor_tensor(o, i, i, ALU.mult))
            vop("vector", lambda e, o=varr[0:1, :], i=bkq.ap[0:1, :], m=msq[0:1, :]:
                e.scalar_tensor_tensor(o, i, 1.0 / H, m, ALU.mult, ALU.subtract),
                waits=bkq.rw(), inc=bkq.ri())
            act(lnv[0:1, :], varr[0:1, :], AF.Ln, bias=epsc[0:1, 0:1],
                waits=bkq.rdone() + [("c_ep", 1)])
            act(rsr[0:1, :], lnv[0:1, :], AF.Exp, scale=-0.5, inc=lnr.inc())
            vop("vector", lambda e, o=mean_b[0:1, :], i=strow[0:1, :]:
                e.tensor_copy(o, i), inc=lnmb.inc())
            vop("vector", lambda e, o=rstd_b[0:1, :], i=rsr[0:1, :]:
                e.tensor_copy(o, i), waits=lnr.w(), inc=lnrb.inc())
            mmop(bkx.ap, ones1, mean_b[0:1, :], True, True,
                 waits=bkx.ww() + lnmb.w() + [("c_o1", 1)], inc=bkx.wi())
            mmop(bkq.ap, ones1, rstd_b[0:1, :], True, True,
                 waits=bkq.ww() + lnrb.w(), inc=bkq.wi())
            act(m_rep, bkx.ap, AF.Copy, waits=bkx.rw(), inc=bkx.ri())
            act(r_rep, bkq.ap, AF.Copy, waits=bkq.rw(), inc=bkq.ri())
            rpw = bkx.rdone() + bkq.rdone()
            for hc in range(HC):
                xc = xn_ap[:, xcol0 + hc * xstride: xcol0 + hc * xstride + 512]
                vop("vector", lambda e, o=xc, m=m_rep:
                    e.tensor_tensor(o, o, m, ALU.subtract), waits=rpw)
                vop("vector", lambda e, o=xc, r=r_rep:
                    e.tensor_tensor(o, o, r, ALU.mult), inc=nrm.inc())
            return nrm.w()

        for q in range(4):
            xw = (banks[2].rdone() + banks[3].rdone() + banks[4].rdone()) if q else []
            nw = ln_pass(d["xT"], q * 512, xnq, 0, 512, xw=xw)
            bkk = banks[2]
            for hc in range(HC):
                mmop(bkk.ap, wkv_s[:, hc * 256:hc * 256 + 128],
                     xnq[:, hc * 512:(hc + 1) * 512], hc == 0, hc == HC - 1,
                     waits=(bkk.ww() + nw + [("c_kv", 16)]) if hc == 0 else [],
                     inc=bkk.wi() if hc == HC - 1 else None)
            act(kT[:, q * 512:(q + 1) * 512], bkk.ap, AF.Copy,
                waits=bkk.rw(), inc=bkk.ri())
            for kc in range(4 * q, 4 * q + 4):
                bkv = banks[3 + (kc % 2)]
                loc = kc * 128 - q * 512
                for hc in range(HC):
                    mmop(bkv.ap[0:128, 0:128],
                         xnq[:, hc * 512 + loc: hc * 512 + loc + 128],
                         wkv_s[:, hc * 256 + 128: hc * 256 + 256],
                         hc == 0, hc == HC - 1,
                         waits=bkv.ww() if hc == 0 else [],
                         inc=bkv.wi() if hc == HC - 1 else None)
                act(V[:, kc * 128:(kc + 1) * 128], bkv.ap[0:128, 0:128], AF.Copy,
                    waits=bkv.rw(), inc=bkv.ri())
        for q in range(2):
            lo_nw = ln_pass(d["xTown"], q * 512, xn_own, q * 512, TOWN)
        for q in range(2):
            bk = banks[6 + q]
            for hc in range(HC):
                mmop(bk.ap[0:64, :], b1_s[:, hc * 64:(hc + 1) * 64],
                     xn_own[:, hc * TOWN + q * 512: hc * TOWN + q * 512 + 512],
                     hc == 0, hc == HC - 1,
                     waits=(bk.ww() + lo_nw + [("c_b1", 16)]) if hc == 0 else [],
                     inc=bk.wi() if hc == HC - 1 else None)
            act(xgT[0:64, q * 512:(q + 1) * 512], bk.ap[0:64, :], AF.Copy,
                waits=bk.rw(), inc=bk.ri())
        xg_done = banks[6].rdone() + banks[7].rdone()
        dma("sync", kv_dram[:, 0:SEQ], kT, ("kvb", 16), waits=banks[2].rdone())
        dma("sync", kv_dram[:, SEQ:2 * SEQ], V, ("kvb", 16),
            waits=banks[3].rdone() + banks[4].rdone())

        abf.pop()
        af32.pop()
        p.barrier("ph2")

        abf.push()
        af32.push()
        au8.push()
        u8g = au8.alloc(4096)
        u8u = au8.alloc(4096)
        hn_gu = au8.alloc(256)
        ln_gu = au8.alloc(256)
        hn_d = au8.alloc(512)
        ln_d = au8.alloc(512)
        d8all = [au8.alloc(2048)]
        repg = abf.alloc(4096)
        repu = abf.alloc(4096)
        deqh = [abf.alloc(16 * 256), abf.alloc(16 * 256)]
        MAXCH = 2 * max(ROUND_BLOCKS)
        hact = abf.alloc(MAXCH * TOWN, tag="hact")
        t1b = [abf.alloc(512), abf.alloc(512)]
        nszg = abf.alloc(1024, tag="nszg")
        nszu = abf.alloc(1024, tag="nszu")
        dszr = abf.alloc(H, tag="dszr")
        xdT = abf.alloc(TOWN, tag="xdT")
        ddeq = [abf.alloc(8 * 512), abf.alloc(8 * 512)]
        repd = abf.alloc(4 * 512)
        gtl = [abf.alloc(512), abf.alloc(512)]
        ldb = af32.alloc(4096)
        ptl = [af32.alloc(512), af32.alloc(512)]
        pin = [af32.alloc(512), af32.alloc(512)]
        sscr = [af32.alloc(512), af32.alloc(512)]

        u8L = [C("u8gL"), C("u8uL")]
        repL = [C("rgL"), C("ruL")]
        dqh = [C("dqh0"), C("dqh1")]
        dqlw = [C("dql0w"), C("dql1w")]
        dql = [C("dql0"), C("dql1")]
        nzc = [C("nzc0"), C("nzc1")]
        nlz = C("nlz")
        dlz = C("dlz")
        dcz = C("dcz")
        d8L = [C("d8L0"), C("d8L1")]
        rdL = C("rdL")
        dqhd = C("dqhd")
        dqldw = C("dqldw")
        dqld = C("dqld")
        ptF = [C("ptF0"), C("ptF1")]
        pinL = [C("pinL0"), C("pinL1")]
        gsl = [C("gsl0"), C("gsl1")]
        gsf = [C("gsf0"), C("gsf1")]

        p.op("gpsimd", lambda e, o=dszr: e.memset(o, 0.0), inc=("dsz0", 1))

        def dequant(lo_eng, u8ap, repap, hi_out, lo_out, hn, ln_,
                    c_h, c_lw, c_l, w0):
            vop("vector", lambda e, o=hn, i=u8ap:
                e.tensor_scalar(o, i, 4, None, ALU.logical_shift_right),
                waits=list(w0))
            vop("vector", lambda e, o=hi_out, i=hn, r=repap:
                e.tensor_tensor(o, i, r, ALU.mult), inc=c_h.inc())
            vop("vector", lambda e, o=ln_, i=u8ap:
                e.tensor_scalar(o, i, 15, None, ALU.bitwise_and),
                waits=c_l.w(), inc=c_lw.inc())
            vop(lo_eng, lambda e, o=lo_out, i=ln_, r=repap:
                e.tensor_tensor(o, i, r, ALU.mult),
                waits=c_lw.w(), inc=c_l.inc())

        rounds = ROUND_BLOCKS[:KB_ROUNDS] if KBISECT >= 3 else []
        full_ffn = (rounds == ROUND_BLOCKS)
        blk0 = 0
        t1_free = [[], []]
        pin_free = [[], []]
        for rnd, nblk in enumerate(rounds):
            nch = 2 * nblk
            ispan = nblk * 256
            gd0 = 4 * blk0
            gspan = 4 * nblk
            dprev = (banks[2].rdone() + banks[3].rdone() +
                     banks[0].rdone() + banks[1].rdone()) if rnd else []
            for wi, (sdr, zdr, nsz) in enumerate(((d["sT_g"], d["zT_g"], nszg),
                                                  (d["sT_u"], d["zT_u"], nszu))):
                lw = nzc[0].w() if wi == 1 else (dcz.w() if rnd else [])
                dma("sync", ldb[0:64, 0:ispan], sdr[:, blk0 * 256: blk0 * 256 + ispan],
                    nlz.inc(16), waits=lw)
                dma("sync", ldb[0:64, ispan:2 * ispan],
                    zdr[:, blk0 * 256: blk0 * 256 + ispan], nlz.inc(16))
                nzw = []
                if rnd:
                    for bb in banks:
                        nzw += bb.wdone()
                vop("vector", lambda e, o=nsz[0:64, 0:ispan], a=ldb[0:64, 0:ispan],
                    b=ldb[0:64, ispan:2 * ispan]: e.tensor_tensor(o, a, b, ALU.mult),
                    waits=nlz.w() + nzw, inc=nzc[wi].inc())
            for hh in range(2):
                dma("sync", ldb[0:gspan, 0:2048],
                    d["sT_d"][gd0:gd0 + gspan, hh * 2048:(hh + 1) * 2048],
                    dlz.inc(16), waits=nzc[1].w() if hh == 0 else dcz.w())
                dma("sync", ldb[0:gspan, 2048:4096],
                    d["zT_d"][gd0:gd0 + gspan, hh * 2048:(hh + 1) * 2048],
                    dlz.inc(16))
                vop("vector", lambda e, o=dszr[0:gspan, hh * 2048:(hh + 1) * 2048],
                    a=ldb[0:gspan, 0:2048], b=ldb[0:gspan, 2048:4096]:
                    e.tensor_tensor(o, a, b, ALU.mult),
                    waits=dlz.w() + [("dsz0", 1)] + dprev, inc=dcz.inc())
            nszw = nzc[0].w() + nzc[1].w()
            dszw = dcz.w()

            for li in range(nblk):
                blk = blk0 + li
                dma("sync", u8g, d["g8"][blk], u8L[0].inc(16),
                    waits=dqh[0].w() + dqlw[0].w())
                dma("sync", u8u, d["up8"][blk], u8L[1].inc(16),
                    waits=dqh[1].w() + dqlw[1].w())
                for gp in range(4):
                    src = stg_g2[gp, blk].flatten().unsqueeze(0)
                    src = src.broadcast_to([32, 16 * 256])
                    dma("gpsimd", repg[32 * gp:32 * gp + 32, :], src, repL[0].inc(16),
                        waits=(dqh[0].w() + dql[0].w() + stg_done) if gp == 0 else ())
                for gp in range(4):
                    src = stg_u2[gp, blk].flatten().unsqueeze(0)
                    src = src.broadcast_to([32, 16 * 256])
                    dma("gpsimd", repu[32 * gp:32 * gp + 32, :], src, repL[1].inc(16),
                        waits=(dqh[1].w() + dql[1].w() + stg_done) if gp == 0 else ())
                for wi in range(2):
                    u8b = u8g if wi == 0 else u8u
                    rep = repg if wi == 0 else repu
                    nsz = nszg if wi == 0 else nszu
                    bk_base = 0 if wi == 0 else 4
                    for ha in range(2):
                        db = ha
                        dqw = [(f"deqF{db}", p.cur(f"deqF{db}")),
                               (u8L[wi].name, p.cur(u8L[wi].name)),
                               (repL[wi].name, p.cur(repL[wi].name))]
                        for jj in range(8):
                            jc = ha * 8 + jj
                            u8t = u8b[:, jc * 256:(jc + 1) * 256]
                            rpt = rep[:, jc * 256:(jc + 1) * 256]
                            dequant("gpsimd", u8t, rpt,
                                    deqh[db][:, (2 * jj) * 256:(2 * jj + 1) * 256],
                                    deqh[db][:, (2 * jj + 1) * 256:(2 * jj + 2) * 256],
                                    hn_gu, ln_gu, dqh[wi], dqlw[wi], dql[wi],
                                    dqw if jj == 0 else ())
                        dqr = dqh[wi].w() + dql[wi].w()
                        for nib in range(2):
                            for half in range(2):
                                bk = banks[bk_base + 2 * nib + half]
                                for jj in range(8):
                                    jc = ha * 8 + jj
                                    for pl in range(2):
                                        xchunk = jc if pl == 0 else 16 + jc
                                        lhs = deqh[db][:, (2 * jj + pl) * 256 + nib * 128:
                                                       (2 * jj + pl) * 256 + nib * 128 + 128]
                                        rhs = xn_own[:, xchunk * TOWN + half * 512:
                                                     xchunk * TOWN + half * 512 + 512]
                                        first = (ha == 0 and jj == 0 and pl == 0)
                                        last_of_half = (nib == 1 and half == 1 and
                                                        jj == 7 and pl == 1)
                                        mmop(bk.ap, lhs, rhs, first, False,
                                             waits=(dqr + bk.ww() + nszw + xg_done)
                                             if (jj == 0 and pl == 0) else [],
                                             inc=(f"deqF{db}", 1) if last_of_half else None)
                                if ha == 1:
                                    lloc = (2 * li + nib) * 128
                                    mmop(bk.ap, nsz[0:64, lloc:lloc + 128],
                                         xgT[0:64, half * 512:half * 512 + 512],
                                         False, True, inc=bk.wi())
                for nib in range(2):
                    cl = 2 * li + nib
                    for half in range(2):
                        gb = banks[2 * nib + half]
                        ub = banks[4 + 2 * nib + half]
                        act(t1b[half], gb.ap, AF.Silu,
                            waits=gb.rw() + t1_free[half], inc=gb.ri())
                        vop("vector", lambda e,
                            o=hact[:, cl * TOWN + half * 512: cl * TOWN + half * 512 + 512],
                            a=t1b[half], b=ub.ap: e.tensor_tensor(o, a, b, ALU.mult),
                            waits=ub.rw() + gb.rdone() + dprev, inc=ub.ri())
                        t1_free[half] = ub.rdone()
            hact_done = []
            for bb in banks[4:8]:
                hact_done += bb.rdone()

            for half in range(2):
                bk = banks[half]
                for cl in range(nch):
                    off = 28 - 4 * (cl // 2)
                    mmop(bk.ap[0:32, :], bbuf_s[:, off:off + 32],
                         hact[:, cl * TOWN + half * 512: cl * TOWN + half * 512 + 512],
                         cl == 0, cl == nch - 1,
                         waits=(bk.ww() + hact_done + [("c_bb", 16)])
                         if cl == 0 else [],
                         inc=bk.wi() if cl == nch - 1 else None)
                act(xdT[0:32, half * 512:half * 512 + 512], bk.ap[0:32, :], AF.Copy,
                    waits=bk.rw() + dprev, inc=bk.ri())
            xdw = banks[0].rdone() + banks[1].rdone()

            for hg in range(8):
                dq = hg % 2
                for gp in range(4):
                    src = stg_d2[gp, hg, blk0:blk0 + nblk, :].flatten().unsqueeze(0)
                    src = src.broadcast_to([32, nblk * 512])
                    dma("gpsimd", repd[32 * gp:32 * gp + 32, 0:nblk * 512], src,
                        rdL.inc(16),
                        waits=(dqhd.w() + dqld.w() + stg_done) if gp == 0 else ())
                ddqf = (banks[2].rdone() + banks[3].rdone() +
                        banks[0].rdone() + banks[1].rdone())
                srcd = d["d8"].rearrange("b p n -> p b n")[:, blk0:blk0 + nblk,
                                                          hg * 512:(hg + 1) * 512]
                dma("sync", d8all[0][:, 0:nblk * 512], srcd,
                    d8L[0].inc(16), waits=dqhd.w() + dqldw.w())
                for li in range(nblk):
                    w0 = d8L[0].w() + rdL.w() + (ddqf if li == 0 else [])
                    dequant("gpsimd", d8all[0][:, li * 512:(li + 1) * 512],
                            repd[:, li * 512:(li + 1) * 512],
                            ddeq[dq][:, (2 * li) * 512:(2 * li + 1) * 512],
                            ddeq[dq][:, (2 * li + 1) * 512:(2 * li + 2) * 512],
                            hn_d, ln_d, dqhd, dqldw, dqld, w0)
                dqr2 = dqhd.w() + dqld.w()
                for ti in range(8):
                    bk = banks[(2, 3, 0, 1)[ti % 4]]
                    for li in range(nblk):
                        for nib in range(2):
                            cl = 2 * li + nib
                            lhs = hact[:, cl * TOWN + ti * 128: cl * TOWN + ti * 128 + 128]
                            rhs = ddeq[dq][:, cl * 512:(cl + 1) * 512]
                            first = (li == 0 and nib == 0)
                            mmop(bk.ap, lhs, rhs, first, False,
                                 waits=(bk.ww() + dqr2 + xdw + dszw) if first else [])
                    mmop(bk.ap, xdT[0:32, ti * 128:ti * 128 + 128],
                         dszr[0:32, hg * 512:(hg + 1) * 512],
                         False, True, inc=bk.wi())
                    dsl = dpart[ti * 128:(ti + 1) * 128, hg * 512:(hg + 1) * 512]
                    pb = (hg * 8 + ti) % 2
                    rbw = (ptF[0].w() + ptF[1].w()) \
                        if (hg == 0 and ti == 0 and rnd > 0) else []
                    if rnd == 0:
                        act(ptl[pb], bk.ap, AF.Copy,
                            waits=bk.rw() + ptF[pb].w(), inc=bk.ri())
                        dma("sync", dsl, ptl[pb], ptF[pb].inc(16),
                            waits=bk.rdone())
                    elif rnd < NRND - 1 or not full_ffn:
                        act(ptl[pb], bk.ap, AF.Copy,
                            waits=bk.rw() + ptF[pb].w(), inc=bk.ri())
                        p.op("gpsimd", lambda e, o=dsl, i=ptl[pb]:
                             e.dma_start(out=o, in_=i, accum_op=ALU.add),
                             waits=bk.rdone() + rbw, inc=ptF[pb].inc(16))
                    else:
                        sb = (hg * 8 + ti) % 2
                        dma("sync", pin[pb], dsl, pinL[pb].inc(16),
                            waits=rbw + pin_free[pb])
                        vop("vector", lambda e, o=sscr[sb], i=bk.ap, m=pin[pb]:
                            e.scalar_tensor_tensor(o, i, 1.0, m, ALU.mult, ALU.add),
                            waits=bk.rw() + pinL[pb].w() + gsl[sb].w(),
                            inc=bk.ri())
                        pin_free[pb] = bk.rdone()
                        act(gtl[sb], sscr[sb], AF.Silu,
                            waits=bk.rdone() + gsf[sb].w(), inc=gsl[sb].inc())
                        dma("sync", gbounce[ti * 128:(ti + 1) * 128,
                                            hg * 512:(hg + 1) * 512],
                            gtl[sb], gsf[sb].inc(16), waits=gsl[sb].w())
            blk0 += nblk

        abf.pop()
        af32.pop()
        au8.pop()
        p.barrier("ph3")

        avT = abf.alloc(NH * TOWN, tag="avT")
        abf.push()
        af32.push()
        kT4 = abf.alloc(SEQ, tag="kT4")
        V4 = abf.alloc(SEQ, tag="V4")
        wqb = [abf.alloc(HC * DH), abf.alloc(HC * DH)]
        qh = [abf.alloc(TOWN, tag="qh"), abf.alloc(TOWN)]
        ptb = [[abf.alloc(512) for _ in range(3)], [abf.alloc(512) for _ in range(3)]]
        rrow_b = abf.alloc(512)
        m2_s = [af32.alloc(2432), af32.alloc(2432)]
        vscr = [[af32.alloc(512), af32.alloc(512)], [af32.alloc(512), af32.alloc(512)]]
        rf = af32.alloc(512)

        kvl = C("kvl")
        wqL = [C("wqL0"), C("wqL1")]
        rrb = C("rrb")
        ptbL = [[C("ptbL00"), C("ptbL01"), C("ptbL02")],
                [C("ptbL10"), C("ptbL11"), C("ptbL12")]]

        dma("sync", kT4, kv_dram[:, 0:SEQ], kvl.inc(16), waits=[("kvb", 32)])
        dma("sync", V4, kv_dram[:, SEQ:2 * SEQ], kvl.inc(16))
        dma("sync", m2_s[0], d["m2q0"], kvl.inc(16))
        dma("sync", m2_s[1], d["m2q1"], kvl.inc(16))
        kvw = kvl.w()

        KC_OF_QB = (8, 16)
        ptb_free = [[[], [], []], [[], [], []]]
        vs_free = [[[], []], [[], []]]
        rf_free = []
        rr_free = []
        for hp in range(NH // 2 if KBISECT >= 4 else 0):
            heads = (2 * hp, 2 * hp + 1)
            for s, h in enumerate(heads):
                wb = h % 2
                dma("gpsimd", wqb[wb], d["wq"][h], wqL[wb].inc(16),
                    waits=(banks[0].wdone() + banks[1].wdone()) if h >= 2 else ())
            sc_bk = [[banks[2], banks[6]], [banks[3], banks[7]]]
            av_bk = [banks[4], banks[5]]
            sum_bk = [banks[0], banks[1]]
            q_ready = []
            for s, h in enumerate(heads):
                wb = h % 2
                qhf = (sc_bk[s][0].wdone() + sc_bk[s][1].wdone()) if hp else []
                for half in range(2):
                    bk = banks[half]
                    for hc in range(HC):
                        mmop(bk.ap, wqb[wb][:, hc * DH:(hc + 1) * DH],
                             xn_own[:, hc * TOWN + half * 512: hc * TOWN + half * 512 + 512],
                             hc == 0, hc == HC - 1,
                             waits=(bk.ww() + wqL[wb].w()) if hc == 0 else [],
                             inc=bk.wi() if hc == HC - 1 else None)
                    act(qh[s][:, half * 512:half * 512 + 512], bk.ap, AF.Copy,
                        scale=sd / slopesf[h], waits=bk.rw() + qhf, inc=bk.ri())
                q_ready = banks[0].rdone() + banks[1].rdone()
            for qb in range(2):
                nkc = KC_OF_QB[qb]
                for kc in range(nkc):
                    for s, h in enumerate(heads):
                        vb = kc % 2
                        tb = kc % 3
                        bk = sc_bk[s][vb]
                        mmop(bk.ap, kT4[:, kc * 128:(kc + 1) * 128],
                             qh[s][:, qb * 512:qb * 512 + 512], True, True,
                             waits=bk.ww() + q_ready + kvw, inc=bk.wi())
                        m2v = m2_s[qb][:, 1920 - kc * 128: 1920 - kc * 128 + 512]
                        vop("vector", lambda e, o=vscr[s][vb], i=bk.ap, m=m2v:
                            e.scalar_tensor_tensor(o, i, 1.0, m, ALU.mult, ALU.add),
                            waits=bk.rw() + vs_free[s][vb], inc=bk.ri())
                        act(ptb[s][tb], vscr[s][vb], AF.Exp, scale=slopesf[h],
                            waits=bk.rdone() + ptb_free[s][tb],
                            inc=ptbL[s][tb].inc())
                        vs_free[s][vb] = ptbL[s][tb].w()
                        ptw = ptbL[s][tb].w()
                        mmop(sum_bk[s].ap[0:1, :], onescol, ptb[s][tb],
                             kc == 0, kc == nkc - 1,
                             waits=ptw + (sum_bk[s].ww() if kc == 0 else []),
                             inc=sum_bk[s].wi() if kc == nkc - 1 else None)
                        lastmm = (kc == nkc - 1)
                        mmop(av_bk[s].ap, V4[:, kc * 128:(kc + 1) * 128], ptb[s][tb],
                             kc == 0, lastmm,
                             waits=ptw + (av_bk[s].ww() if kc == 0 else []),
                             inc=av_bk[s].wi() if lastmm else (f"ptbF{s}{tb}", 1))
                        if lastmm:
                            ptb_free[s][tb] = [(f"{av_bk[s].name}W", av_bk[s].w)]
                        else:
                            ptb_free[s][tb] = [(f"ptbF{s}{tb}", p.cur(f"ptbF{s}{tb}"))]
                for s, h in enumerate(heads):
                    vop("vector", lambda e, o=rf[0:1, :], i=sum_bk[s].ap[0:1, :]:
                        e.reciprocal(o, i),
                        waits=sum_bk[s].rw() + rr_free, inc=sum_bk[s].ri())
                    vop("vector", lambda e, o=rrow_b[0:1, :], i=rf[0:1, :]:
                        e.tensor_copy(o, i), inc=rrb.inc())
                    rep_bk = sc_bk[s][0]
                    mmop(rep_bk.ap, ones1, rrow_b[0:1, :], True, True,
                         waits=rep_bk.ww() + rrb.w(), inc=rep_bk.wi())
                    rr_free = [(f"{rep_bk.name}W", rep_bk.w)]
                    act(rf, rep_bk.ap, AF.Copy,
                        waits=rep_bk.rw() + rf_free, inc=rep_bk.ri())
                    vop("vector", lambda e,
                        o=avT[:, h * TOWN + qb * 512: h * TOWN + qb * 512 + 512],
                        a=av_bk[s].ap, b=rf: e.tensor_tensor(o, a, b, ALU.mult),
                        waits=av_bk[s].rw() + rep_bk.rdone(), inc=av_bk[s].ri())
                    rf_free = [(f"{av_bk[s].name}R", av_bk[s].r)]
        av_done = banks[4].rdone() + banks[5].rdone()

        abf.pop()
        af32.pop()
        p.barrier("ph4")

        abf.push()
        af32.push()
        wob = abf.alloc(HC * 512)
        gio = [abf.alloc(512), abf.alloc(512)]
        xio = [af32.alloc(512), af32.alloc(512)]
        oscr = [af32.alloc(512), af32.alloc(512)]
        oout = [af32.alloc(512), af32.alloc(512)]
        woL = C("woL")
        giL = [C("giL0"), C("giL1")]
        xiL = [C("xiL0"), C("xiL1")]
        ooL = [C("ooL0"), C("ooL1")]
        ooF = [C("ooF0"), C("ooF1")]
        g_all = (gsf[0].w() + gsf[1].w()) if full_ffn else []
        gi_free = [[], []]
        os_free = [[], []]

        if KBISECT < 5:
            p.op("gpsimd", lambda e, o=oout[0]: e.memset(o, 0.0), inc=("stub", 1))
            for hg in range(8):
                for ti in range(8):
                    dma("sync", out_d[ti * 128:(ti + 1) * 128, hg * 512:(hg + 1) * 512],
                        oout[0], ("stubo", 16), waits=[("stub", 1)])
            p.op("sync", lambda e: e.nop(), waits=[("stubo", p.cur("stubo"))])
        for hg in range(8 if KBISECT >= 5 else 0):
            dma("gpsimd", wob, d["wo"][hg], woL.inc(16),
                waits=(banks[0].wdone() + banks[1].wdone() +
                       banks[2].wdone() + banks[3].wdone()) if hg else ())
            for ti in range(8):
                it = hg * 8 + ti
                gb2 = it % 2
                dma("sync", gio[gb2], gbounce[ti * 128:(ti + 1) * 128, hg * 512:(hg + 1) * 512],
                    giL[gb2].inc(16), waits=gi_free[gb2] + g_all)
                xb = it % 2
                dma("sync", xio[xb], d["xown"][ti * 128:(ti + 1) * 128, hg * 512:(hg + 1) * 512],
                    xiL[xb].inc(16), waits=ooL[xb].w())
                bk = banks[ti % 4]
                for hh in range(NH):
                    mmop(bk.ap, avT[:, hh * TOWN + ti * 128: hh * TOWN + ti * 128 + 128],
                         wob[:, hh * 512:(hh + 1) * 512], hh == 0, hh == NH - 1,
                         waits=(bk.ww() + woL.w() + av_done) if hh == 0 else [],
                         inc=bk.wi() if hh == NH - 1 else None)
                ob = it % 2
                vop("vector", lambda e, o=oscr[ob], a=bk.ap, b=gio[gb2]:
                    e.tensor_tensor(o, a, b, ALU.mult),
                    waits=bk.rw() + giL[gb2].w() + os_free[ob], inc=bk.ri())
                gi_free[gb2] = bk.rdone()
                vop("vector", lambda e, o=oout[ob], a=oscr[ob], b=xio[xb]:
                    e.tensor_tensor(o, a, b, ALU.add),
                    waits=bk.rdone() + xiL[xb].w() + ooF[ob].w(), inc=ooL[ob].inc())
                os_free[ob] = ooL[ob].w()
                dma("sync", out_d[ti * 128:(ti + 1) * 128, hg * 512:(hg + 1) * 512],
                    oout[ob], ooF[ob].inc(16), waits=ooL[ob].w())
        if KB_DBG:
            dma("sync", dbg_xn, xn_own, ("dbgs", 16))
            dma("sync", dbg_av, avT, ("dbgs", 16))
            p.op("sync", lambda e: e.nop(), waits=[("dbgs", p.cur("dbgs"))])
        if KBISECT >= 5:
            p.op("sync", lambda e: e.nop(), waits=ooF[0].w() + ooF[1].w())

        with contextlib.ExitStack() as ss:
            sems = {n: ss.enter_context(nc.semaphore(n)) for n in p.plan}
            with nc.Block() as blk:
                @blk.tensor
                def _(e):
                    p.emit("tensor", e, sems)

                @blk.vector
                def _(e):
                    p.emit("vector", e, sems)

                @blk.scalar
                def _(e):
                    p.emit("scalar", e, sems)

                @blk.gpsimd
                def _(e):
                    p.emit("gpsimd", e, sems)

                @blk.sync
                def _(e):
                    p.emit("sync", e, sems)
        pstack.close()
    return nc


def kernel(**inputs):
    maps, own_blocks = host_prep(inputs)
    nc = build_program()
    res = run_bass_kernel_spmd(nc, maps, core_ids=list(range(NCORES)))
    out = np.zeros((4, SEQ, H), dtype=np.float32)
    for core in range(NCORES):
        seq = core // 2
        blocks = own_blocks[core % 2]
        o = np.asarray(res.results[core]["out"])
        out[seq, blocks[0][0]:blocks[0][1], :] = o[0:512, :]
        out[seq, blocks[1][0]:blocks[1][1], :] = o[512:1024, :]
    return out


# version 16
# speedup vs baseline: 1.1381x; 1.1381x over previous
import contextlib
import os
import numpy as np
import ml_dtypes
import concourse.bass as bass
import concourse.mybir as mybir
from concourse.bass_utils import run_bass_kernel_spmd

F32 = mybir.dt.float32
BF16 = mybir.dt.bfloat16
U8 = mybir.dt.uint8
AF = mybir.ActivationFunctionType
ALU = mybir.AluOpType
BFD = ml_dtypes.bfloat16

H = 4096
SEQ = 2048
TOWN = 1024
NH = 32
DH = 128
II = 11008
NB = 43
HC = 32
NCORES = 8
ROUND_BLOCKS = [4] * 10 + [3]
NRND = len(ROUND_BLOCKS)
MASK_NEG = -1.0e9
EPS = 1e-5
KBISECT = int(os.environ.get('KBISECT', '5'))
KB_ROUNDS = int(os.environ.get('KB_ROUNDS', '99'))
KB_DBG = int(os.environ.get('KB_DBG', '0'))


def _slopes():
    start = 2.0 ** (-(2.0 ** (-(np.log2(NH) - 3))))
    return np.array([start ** (i + 1) for i in range(NH)], dtype=np.float32)


def _plane_perm():
    return np.concatenate([np.arange(0, H, 2), np.arange(1, H, 2)])


def _ic_order():
    cols = []
    for b in range(NB):
        cols.append(np.arange(256 * b, 256 * b + 256, 2))
        cols.append(np.arange(256 * b + 1, 256 * b + 256, 2))
    return np.concatenate(cols)


def _u8_of_i32(a):
    return np.ascontiguousarray(
        a.view(np.uint8).reshape(a.shape[0], a.shape[1], 4)[:, :, 0])


OFFS = {}
ENGINES = ("tensor", "vector", "scalar", "gpsimd", "sync")


class Prog:
    def __init__(self):
        self.ops = {e: [] for e in ENGINES}
        self.plan = {}
        self.ensured = {}

    def note(self, s):
        if s not in self.plan:
            self.plan[s] = 0

    def cur(self, s):
        self.note(s)
        return self.plan[s]

    def op(self, eng, fn, waits=(), inc=None):
        rw = []
        for (s, v) in waits:
            self.note(s)
            assert v <= self.plan[s], f"wait on future {s}: {v} > {self.plan[s]}"
            if v <= 0:
                continue
            k = (eng, s)
            if self.ensured.get(k, 0) >= v:
                continue
            self.ensured[k] = v
            rw.append((s, v))
        if inc is not None:
            s, amt = inc
            self.note(s)
            self.plan[s] += amt
        self.ops[eng].append((rw, fn, inc))

    def barrier(self, name):
        for e in ENGINES:
            self.op(e, lambda en: en.nop(), inc=(name, 1))
        tgt = self.cur(name)
        for e in ENGINES:
            self.op(e, lambda en: en.nop(), waits=[(name, tgt)])

    def emit(self, eng, e, sems):
        for (waits, fn, inc) in self.ops[eng]:
            for (s, v) in waits:
                e.wait_ge(sems[s], v)
            ins = fn(e)
            if inc is not None:
                ins.then_inc(sems[inc[0]], inc[1])


class Arena:
    def __init__(self, ap, total, name):
        self.ap = ap
        self.total = total
        self.off = 0
        self.marks = []
        self.name = name
        self.peak = 0

    def alloc(self, n, align=128, tag=None):
        o = (self.off + align - 1) // align * align
        assert o + n <= self.total, f"{self.name} overflow: {o}+{n} > {self.total}"
        self.off = o + n
        self.peak = max(self.peak, self.off)
        if tag is not None:
            OFFS[tag] = (self.name, o, n)
        return self.ap[:, o:o + n]

    def push(self):
        self.marks.append(self.off)

    def pop(self):
        self.off = self.marks.pop()


class Ctr:
    def __init__(self, prog, name):
        self.p, self.name = prog, name

    def inc(self, amt=1):
        return (self.name, amt)

    def w(self):
        return [(self.name, self.p.cur(self.name))]


class Bank:
    def __init__(self, name, ap):
        self.name, self.ap = name, ap
        self.w = 0
        self.r = 0

    def ww(self):
        return [] if self.r == 0 else [(f"{self.name}R", self.r)]

    def wi(self):
        self.w += 1
        return (f"{self.name}W", 1)

    def rw(self):
        return [(f"{self.name}W", self.w)]

    def ri(self):
        self.r += 1
        return (f"{self.name}R", 1)

    def rdone(self):
        return [] if self.r == 0 else [(f"{self.name}R", self.r)]

    def wdone(self):
        return [] if self.w == 0 else [(f"{self.name}W", self.w)]


def host_prep(inp):
    pp = _plane_perm()
    ico = _ic_order()
    x = np.asarray(inp["L0_attn_hidden_states"], dtype=np.float32)
    Wq = np.asarray(inp["L0_attn_Wq"], dtype=np.float32)
    Wkv = np.asarray(inp["L0_attn_Wkv"], dtype=np.float32)
    Wo = np.asarray(inp["L0_attn_Wo"], dtype=np.float32)

    wq_t = np.ascontiguousarray(Wq[pp, :].reshape(HC, 128, NH, DH).transpose(2, 1, 0, 3))
    wkv_t = np.ascontiguousarray(Wkv[pp, :].reshape(HC, 128, 256).transpose(1, 0, 2))
    wo_t = np.ascontiguousarray(Wo.reshape(HC, 128, 8, 512).transpose(2, 1, 0, 3))

    def prep_gu(wname):
        u8 = _u8_of_i32(np.asarray(inp[wname]))
        u8r = np.ascontiguousarray(u8.T)[:, ico]
        return np.ascontiguousarray(u8r.reshape(16, 128, NB, 256).transpose(2, 1, 0, 3))

    g8 = prep_gu("L0_ffn_gate_Wq")
    up8 = prep_gu("L0_ffn_up_Wq")
    d8 = _u8_of_i32(np.asarray(inp["L0_ffn_down_Wq"]))
    d8t = np.ascontiguousarray(d8.T.reshape(NB, 128, H))

    sT_g = np.ascontiguousarray(np.asarray(inp["L0_ffn_gate_scale"], np.float32).T[:, ico])
    zT_g = np.ascontiguousarray(np.asarray(inp["L0_ffn_gate_zero"], np.float32).T[:, ico])
    sT_u = np.ascontiguousarray(np.asarray(inp["L0_ffn_up_scale"], np.float32).T[:, ico])
    zT_u = np.ascontiguousarray(np.asarray(inp["L0_ffn_up_zero"], np.float32).T[:, ico])
    sT_d = np.ascontiguousarray(np.asarray(inp["L0_ffn_down_scale"], np.float32).T)
    zT_d = np.ascontiguousarray(np.asarray(inp["L0_ffn_down_zero"], np.float32).T)

    B1 = np.zeros((H, 64), dtype=BFD)
    B1[np.arange(H), pp // 64] = -1.0
    b1_t = np.ascontiguousarray(B1.reshape(HC, 128, 64).transpose(1, 0, 2))

    Bb = np.zeros((128, 60), dtype=BFD)
    for pr in range(128):
        Bb[pr, 28 + pr // 32] = -1.0

    shared = dict(wq=wq_t, wkv=wkv_t, wo=wo_t, g8=g8, up8=up8, d8=d8t,
                  sT_g=sT_g, zT_g=zT_g, sT_u=sT_u, zT_u=zT_u, sT_d=sT_d,
                  zT_d=zT_d, b1=b1_t, bbuf=Bb)

    own_blocks = [[(0, 512), (1536, 2048)], [(512, 1024), (1024, 1536)]]
    maps = []
    for core in range(NCORES):
        seq = core // 2
        blocks = own_blocks[core % 2]
        own_idx = np.concatenate([np.arange(a, b) for (a, b) in blocks])
        xT = np.ascontiguousarray(x[seq].T[pp, :])
        xTown = np.ascontiguousarray(xT[:, own_idx])
        xown = np.ascontiguousarray(x[seq][own_idx, :])
        m2 = []
        for (a, _b) in blocks:
            c = np.arange(2432)
            pr = np.arange(128)
            dd = a + c[None, :] - 1920 - pr[:, None]
            m2.append(np.where(dd >= 0, -dd, MASK_NEG).astype(np.float32))
        m = dict(shared)
        m.update(xT=xT, xTown=xTown, xown=xown, m2q0=m2[0], m2q1=m2[1])
        maps.append(m)
    return maps, own_blocks


def build_program():
    slopesf = [float(s) for s in _slopes()]
    sd = float(1.0 / np.sqrt(DH))
    nc = bass.Bass("TRN2", target_bir_lowering=False)

    d = {}
    for name, shape, dt in [
        ("xT", [H, SEQ], F32), ("xTown", [H, TOWN], F32), ("xown", [TOWN, H], F32),
        ("m2q0", [128, 2432], F32), ("m2q1", [128, 2432], F32),
        ("wq", [NH, 128, HC, DH], F32), ("wkv", [128, HC, 256], F32),
        ("wo", [8, 128, HC, 512], F32),
        ("g8", [NB, 128, 16, 256], U8), ("up8", [NB, 128, 16, 256], U8),
        ("d8", [NB, 128, H], U8),
        ("sT_g", [64, II], F32), ("zT_g", [64, II], F32),
        ("sT_u", [64, II], F32), ("zT_u", [64, II], F32),
        ("sT_d", [172, H], F32), ("zT_d", [172, H], F32),
        ("b1", [128, HC, 64], BF16), ("bbuf", [128, 60], BF16),
    ]:
        d[name] = nc.dram_tensor(name, shape, dt, kind="ExternalInput").ap()
    out_d = nc.dram_tensor("out", [TOWN, H], F32, kind="ExternalOutput").ap()
    kdbg = "ExternalOutput" if KB_DBG else "Internal"
    dpart = nc.dram_tensor("dpart", [TOWN, H], F32, kind=kdbg).ap()
    gbounce = nc.dram_tensor("gbounce", [TOWN, H], BF16, kind=kdbg).ap()
    kv_dram = nc.dram_tensor("kv_dram", [128, 2 * SEQ], BF16, kind=kdbg).ap()
    stg_g = nc.dram_tensor("stg_g", [64, II], BF16, kind="Internal").ap()
    stg_u = nc.dram_tensor("stg_u", [64, II], BF16, kind="Internal").ap()
    stg_d = nc.dram_tensor("stg_d", [172, H], BF16, kind="Internal").ap()
    stg_g2 = nc.dram_tensor("stg_g2", [4, NB, 16, 256], BF16, kind="Internal").ap()
    stg_u2 = nc.dram_tensor("stg_u2", [4, NB, 16, 256], BF16, kind="Internal").ap()
    stg_d2 = nc.dram_tensor("stg_d2", [4, 8, NB, 512], BF16, kind="Internal").ap()
    if KB_DBG:
        dbg_xn = nc.dram_tensor("dbg_xn", [128, HC * TOWN], BF16, kind="ExternalOutput").ap()
        dbg_av = nc.dram_tensor("dbg_av", [128, NH * TOWN], BF16, kind="ExternalOutput").ap()

    p = Prog()

    BF_COLS = 84992
    F32_COLS = 7424
    U8_COLS = 12288

    with (
        nc.sbuf_tensor("a_bf", [128, BF_COLS], BF16) as a_bf_t,
        nc.sbuf_tensor("a_f32", [128, F32_COLS], F32) as a_f32_t,
        nc.sbuf_tensor("a_u8", [128, U8_COLS], U8) as a_u8_t,
        nc.sbuf_tensor("ones1", [1, 128], BF16) as ones1_t,
        nc.sbuf_tensor("onescol", [128, 1], BF16) as onescol_t,
        nc.sbuf_tensor("epsc", [128, 1], F32) as epsc_t,
    ):
        abf = Arena(a_bf_t[:], BF_COLS, "bf16")
        af32 = Arena(a_f32_t[:], F32_COLS, "f32")
        au8 = Arena(a_u8_t[:], U8_COLS, "u8")
        ones1 = ones1_t[:]
        onescol = onescol_t[:]
        epsc = epsc_t[:]

        pstack = contextlib.ExitStack()
        banks = []
        for i in range(8):
            t = pstack.enter_context(nc.psum_tensor(f"pb{i}", [128, 512], F32))
            banks.append(Bank(f"b{i}", t[:]))

        def dma(eng, out, in_, inc, waits=()):
            p.op(eng, lambda e, o=out, i=in_: e.dma_start(out=o, in_=i),
                 waits=waits, inc=inc)

        def mmop(bank_ap, lhsT, rhs, start, stop, waits=(), inc=None):
            p.op("tensor", lambda e, o=bank_ap, l=lhsT, r=rhs, s1=start, s2=stop:
                 e.matmul(o, l, r, start=s1, stop=s2), waits=waits, inc=inc)

        def act(out, in_, func, bias=None, scale=None, waits=(), inc=None):
            kw = {}
            if bias is not None:
                kw["bias"] = bias
            if scale is not None:
                kw["scale"] = scale
            p.op("scalar", lambda e, o=out, i=in_, f=func, k=dict(kw):
                 e.activation(o, i, f, **k), waits=waits, inc=inc)

        def vop(eng, fn, waits=(), inc=None):
            p.op(eng, fn, waits=waits, inc=inc)

        def C(name):
            return Ctr(p, name)

        bbuf_s = abf.alloc(64)
        xn_own = abf.alloc(HC * TOWN, tag="xn_own")
        xgT = abf.alloc(TOWN, tag="xgT")

        dma("sync", bbuf_s[:, 0:60], d["bbuf"], ("c_bb", 16))
        abf.push()
        stgb = abf.alloc(II)
        stgc = C("stgc")
        stgo = C("stgo")
        stgo_h2w = []
        for snm, ddr, dst, rows in (("sg", d["sT_g"], stg_g, 64),
                                    ("su", d["sT_u"], stg_u, 64),
                                    ("sd", d["sT_d"], stg_d, 172)):
            if rows == 64:
                dma("gpsimd", stgb[0:64, 0:II], ddr, stgc.inc(16), waits=stgo.w())
                dma("sync", dst, stgb[0:64, 0:II], stgo.inc(16), waits=stgc.w())
            else:
                dma("gpsimd", stgb[0:128, 0:H], ddr[0:128, :], stgc.inc(16),
                    waits=stgo.w())
                dma("gpsimd", stgb[0:44, H:2 * H], ddr[128:172, :], stgc.inc(16))
                dma("sync", dst[0:128, :], stgb[0:128, 0:H], stgo.inc(16),
                    waits=stgc.w())
                dma("sync", dst[128:172, :], stgb[0:44, H:2 * H], stgo.inc(16))
        h2t = abf.alloc(4096)
        h2c = C("h2c")
        for gp in range(4):
            for wsrc, wdst, pat in (
                    (stg_g, stg_g2, "gu"), (stg_u, stg_u2, "gu"), (stg_d, stg_d2, "d")):
                if pat == "gu":
                    src2 = wsrc.rearrange("(jc g) (blk i) -> g blk jc i", g=4, i=256)[gp]
                    dst2 = wdst[gp]
                else:
                    src2 = wsrc.rearrange("(blk g) (hg i) -> g blk hg i", g=4, i=512)[gp]
                    dst2 = wdst[gp].rearrange("hg blk i -> blk hg i")
                dma("gpsimd", h2t[0:NB, :], src2, h2c.inc(16),
                    waits=stgo.w() + stgo_h2w)
                dma("sync", dst2, h2t[0:NB, :], stgo.inc(16), waits=h2c.w())
                stgo_h2w = stgo.w()
        stg_done = stgo.w()
        p.barrier("ph0")
        abf.pop()
        p.op("gpsimd", lambda e: e.memset(ones1, 1.0), inc=("c_o1", 1))
        p.op("gpsimd", lambda e, o=epsc: e.memset(o, EPS), inc=("c_ep", 1))
        p.op("gpsimd", lambda e: e.memset(onescol, 1.0), inc=("c_oc", 1))

        abf.push()
        af32.push()
        wkv_s = abf.alloc(HC * 256)
        b1_s = abf.alloc(HC * 64)
        kT = abf.alloc(SEQ, tag="kT")
        V = abf.alloc(SEQ, tag="V")
        xnq = abf.alloc(HC * 512, tag="xnq")
        sqb = [abf.alloc(512), abf.alloc(512)]
        mean_b = abf.alloc(512)
        rstd_b = abf.alloc(512)
        m_rep = abf.alloc(512)
        r_rep = abf.alloc(512)
        xf = [af32.alloc(512) for _ in range(3)]
        xf_free = [[], [], []]
        strow = af32.alloc(512)
        msq = af32.alloc(512)
        varr = af32.alloc(512)
        lnv = af32.alloc(512)
        rsr = af32.alloc(512)

        dma("gpsimd", wkv_s, d["wkv"], ("c_kv", 16), waits=stg_done)
        dma("sync", b1_s, d["b1"], ("c_b1", 16), waits=stg_done)

        lnx = C("lnx")
        sqc = C("sqc")
        lnr = C("lnr")
        lnmb = C("lnmb")
        lnrb = C("lnrb")
        nrm = C("nrm")

        def ln_pass(xdram, cs, xn_ap, xcol0, xstride, xw=()):
            bkx, bkq = banks[0], banks[1]
            prev_b1w = bkq.wdone()
            xbc_at = []
            for hc in range(HC):
                b3 = hc % 3
                dma("sync", xf[b3], xdram[hc * 128:(hc + 1) * 128, cs:cs + 512],
                    (f"lnx{b3}", 16), waits=list(xw) + xf_free[b3])
                xc = xn_ap[:, xcol0 + hc * xstride: xcol0 + hc * xstride + 512]
                act(xc, xf[b3], AF.Copy,
                    waits=[(f"lnx{b3}", p.cur(f"lnx{b3}"))] + prev_b1w + list(xw),
                    inc=("xbc", 1))
                xf_free[b3] = [("xbc", p.cur("xbc"))]
                xbc_at.append(p.cur("xbc"))
            for hc in range(HC):
                b = hc % 2
                xc = xn_ap[:, xcol0 + hc * xstride: xcol0 + hc * xstride + 512]
                sqfree = [(f"sqF{b}", p.cur(f"sqF{b}"))]
                act(sqb[b], xc, AF.Square,
                    waits=sqfree, inc=sqc.inc())
                mmop(bkx.ap[0:1, :], onescol, xc, hc == 0, hc == HC - 1,
                     waits=(bkx.ww() + [("c_oc", 1)] if hc == 0 else []) +
                     [("xbc", xbc_at[hc])],
                     inc=bkx.wi() if hc == HC - 1 else None)
                mi = (f"sqF{b}", 1) if hc != HC - 1 else bkq.wi()
                mmop(bkq.ap[0:1, :], onescol, sqb[b], hc == 0, hc == HC - 1,
                     waits=sqc.w() + (bkq.ww() if hc == 0 else []), inc=mi)
            vop("vector", lambda e, o=strow[0:1, :], i=bkx.ap[0:1, :]:
                e.tensor_scalar(o, i, 1.0 / H, None, ALU.mult),
                waits=bkx.rw(), inc=bkx.ri())
            vop("vector", lambda e, o=msq[0:1, :], i=strow[0:1, :]:
                e.tensor_tensor(o, i, i, ALU.mult))
            vop("vector", lambda e, o=varr[0:1, :], i=bkq.ap[0:1, :], m=msq[0:1, :]:
                e.scalar_tensor_tensor(o, i, 1.0 / H, m, ALU.mult, ALU.subtract),
                waits=bkq.rw(), inc=bkq.ri())
            act(lnv[0:1, :], varr[0:1, :], AF.Ln, bias=epsc[0:1, 0:1],
                waits=bkq.rdone() + [("c_ep", 1)])
            act(rsr[0:1, :], lnv[0:1, :], AF.Exp, scale=-0.5, inc=lnr.inc())
            vop("vector", lambda e, o=mean_b[0:1, :], i=strow[0:1, :]:
                e.tensor_copy(o, i), inc=lnmb.inc())
            vop("vector", lambda e, o=rstd_b[0:1, :], i=rsr[0:1, :]:
                e.tensor_copy(o, i), waits=lnr.w(), inc=lnrb.inc())
            mmop(bkx.ap, ones1, mean_b[0:1, :], True, True,
                 waits=bkx.ww() + lnmb.w() + [("c_o1", 1)], inc=bkx.wi())
            mmop(bkq.ap, ones1, rstd_b[0:1, :], True, True,
                 waits=bkq.ww() + lnrb.w(), inc=bkq.wi())
            act(m_rep, bkx.ap, AF.Copy, waits=bkx.rw(), inc=bkx.ri())
            act(r_rep, bkq.ap, AF.Copy, waits=bkq.rw(), inc=bkq.ri())
            rpw = bkx.rdone() + bkq.rdone()
            for hc in range(HC):
                xc = xn_ap[:, xcol0 + hc * xstride: xcol0 + hc * xstride + 512]
                vop("vector", lambda e, o=xc, m=m_rep:
                    e.tensor_tensor(o, o, m, ALU.subtract), waits=rpw)
                vop("vector", lambda e, o=xc, r=r_rep:
                    e.tensor_tensor(o, o, r, ALU.mult), inc=nrm.inc())
            return nrm.w()

        for q in range(4):
            xw = (banks[2].rdone() + banks[3].rdone() + banks[4].rdone()) if q else []
            nw = ln_pass(d["xT"], q * 512, xnq, 0, 512, xw=xw)
            bkk = banks[2]
            for hc in range(HC):
                mmop(bkk.ap, wkv_s[:, hc * 256:hc * 256 + 128],
                     xnq[:, hc * 512:(hc + 1) * 512], hc == 0, hc == HC - 1,
                     waits=(bkk.ww() + nw + [("c_kv", 16)]) if hc == 0 else [],
                     inc=bkk.wi() if hc == HC - 1 else None)
            act(kT[:, q * 512:(q + 1) * 512], bkk.ap, AF.Copy,
                waits=bkk.rw(), inc=bkk.ri())
            for kc in range(4 * q, 4 * q + 4):
                bkv = banks[3 + (kc % 2)]
                loc = kc * 128 - q * 512
                for hc in range(HC):
                    mmop(bkv.ap[0:128, 0:128],
                         xnq[:, hc * 512 + loc: hc * 512 + loc + 128],
                         wkv_s[:, hc * 256 + 128: hc * 256 + 256],
                         hc == 0, hc == HC - 1,
                         waits=bkv.ww() if hc == 0 else [],
                         inc=bkv.wi() if hc == HC - 1 else None)
                act(V[:, kc * 128:(kc + 1) * 128], bkv.ap[0:128, 0:128], AF.Copy,
                    waits=bkv.rw(), inc=bkv.ri())
        for q in range(2):
            lo_nw = ln_pass(d["xTown"], q * 512, xn_own, q * 512, TOWN)
        for q in range(2):
            bk = banks[6 + q]
            for hc in range(HC):
                mmop(bk.ap[0:64, :], b1_s[:, hc * 64:(hc + 1) * 64],
                     xn_own[:, hc * TOWN + q * 512: hc * TOWN + q * 512 + 512],
                     hc == 0, hc == HC - 1,
                     waits=(bk.ww() + lo_nw + [("c_b1", 16)]) if hc == 0 else [],
                     inc=bk.wi() if hc == HC - 1 else None)
            act(xgT[0:64, q * 512:(q + 1) * 512], bk.ap[0:64, :], AF.Copy,
                waits=bk.rw(), inc=bk.ri())
        xg_done = banks[6].rdone() + banks[7].rdone()
        dma("sync", kv_dram[:, 0:SEQ], kT, ("kvb", 16), waits=banks[2].rdone())
        dma("sync", kv_dram[:, SEQ:2 * SEQ], V, ("kvb", 16),
            waits=banks[3].rdone() + banks[4].rdone())

        abf.pop()
        af32.pop()
        p.barrier("ph2")

        abf.push()
        af32.push()
        au8.push()
        u8g = au8.alloc(4096)
        u8u = au8.alloc(4096)
        hn_gu = au8.alloc(256)
        ln_gu = au8.alloc(256)
        hn_d = au8.alloc(512)
        ln_d = au8.alloc(512)
        d8all = [au8.alloc(2048)]
        repg = abf.alloc(4096)
        repu = abf.alloc(4096)
        deqh = [abf.alloc(16 * 256), abf.alloc(16 * 256)]
        MAXCH = 2 * max(ROUND_BLOCKS)
        hact = abf.alloc(MAXCH * TOWN, tag="hact")
        t1b = [abf.alloc(512), abf.alloc(512)]
        nszg = abf.alloc(1024, tag="nszg")
        nszu = abf.alloc(1024, tag="nszu")
        dszr = abf.alloc(H, tag="dszr")
        xdT = abf.alloc(TOWN, tag="xdT")
        ddeq = [abf.alloc(8 * 512), abf.alloc(8 * 512)]
        repd = abf.alloc(4 * 512)
        gtl = [abf.alloc(512), abf.alloc(512)]
        ldb = af32.alloc(4096)
        ptl = [af32.alloc(512), af32.alloc(512)]
        pin = [af32.alloc(512), af32.alloc(512)]
        sscr = [af32.alloc(512), af32.alloc(512)]

        u8L = [C("u8gL"), C("u8uL")]
        repL = [C("rgL"), C("ruL")]
        dqh = [C("dqh0"), C("dqh1")]
        dqlw = [C("dql0w"), C("dql1w")]
        dql = [C("dql0"), C("dql1")]
        nzc = [C("nzc0"), C("nzc1")]
        nlz = C("nlz")
        dlz = C("dlz")
        dcz = C("dcz")
        d8L = [C("d8L0"), C("d8L1")]
        rdL = C("rdL")
        dqhd = C("dqhd")
        dqldw = C("dqldw")
        dqld = C("dqld")
        ptF = [C("ptF0"), C("ptF1")]
        pinL = [C("pinL0"), C("pinL1")]
        gsl = [C("gsl0"), C("gsl1")]
        gsf = [C("gsf0"), C("gsf1")]

        p.op("gpsimd", lambda e, o=dszr: e.memset(o, 0.0), inc=("dsz0", 1))

        def dequant(lo_eng, u8ap, repap, hi_out, lo_out, hn, ln_,
                    c_h, c_lw, c_l, w0):
            vop("vector", lambda e, o=hn, i=u8ap:
                e.tensor_scalar(o, i, 4, None, ALU.logical_shift_right),
                waits=list(w0))
            vop("vector", lambda e, o=hi_out, i=hn, r=repap:
                e.tensor_tensor(o, i, r, ALU.mult), inc=c_h.inc())
            vop("vector", lambda e, o=ln_, i=u8ap:
                e.tensor_scalar(o, i, 15, None, ALU.bitwise_and),
                waits=c_l.w(), inc=c_lw.inc())
            vop(lo_eng, lambda e, o=lo_out, i=ln_, r=repap:
                e.tensor_tensor(o, i, r, ALU.mult),
                waits=c_lw.w(), inc=c_l.inc())

        rounds = ROUND_BLOCKS[:KB_ROUNDS] if KBISECT >= 3 else []
        full_ffn = (rounds == ROUND_BLOCKS)
        blk0 = 0
        t1_free = [[], []]
        pin_free = [[], []]
        for rnd, nblk in enumerate(rounds):
            nch = 2 * nblk
            ispan = nblk * 256
            gd0 = 4 * blk0
            gspan = 4 * nblk
            dprev = (banks[2].rdone() + banks[3].rdone() +
                     banks[0].rdone() + banks[1].rdone()) if rnd else []
            for wi, (sdr, zdr, nsz) in enumerate(((d["sT_g"], d["zT_g"], nszg),
                                                  (d["sT_u"], d["zT_u"], nszu))):
                lw = nzc[0].w() if wi == 1 else (dcz.w() if rnd else [])
                dma("sync", ldb[0:64, 0:ispan], sdr[:, blk0 * 256: blk0 * 256 + ispan],
                    nlz.inc(16), waits=lw)
                dma("sync", ldb[0:64, ispan:2 * ispan],
                    zdr[:, blk0 * 256: blk0 * 256 + ispan], nlz.inc(16))
                nzw = []
                if rnd:
                    for bb in banks:
                        nzw += bb.wdone()
                vop("vector", lambda e, o=nsz[0:64, 0:ispan], a=ldb[0:64, 0:ispan],
                    b=ldb[0:64, ispan:2 * ispan]: e.tensor_tensor(o, a, b, ALU.mult),
                    waits=nlz.w() + nzw, inc=nzc[wi].inc())
            for hh in range(2):
                dma("sync", ldb[0:gspan, 0:2048],
                    d["sT_d"][gd0:gd0 + gspan, hh * 2048:(hh + 1) * 2048],
                    dlz.inc(16), waits=nzc[1].w() if hh == 0 else dcz.w())
                dma("sync", ldb[0:gspan, 2048:4096],
                    d["zT_d"][gd0:gd0 + gspan, hh * 2048:(hh + 1) * 2048],
                    dlz.inc(16))
                vop("vector", lambda e, o=dszr[0:gspan, hh * 2048:(hh + 1) * 2048],
                    a=ldb[0:gspan, 0:2048], b=ldb[0:gspan, 2048:4096]:
                    e.tensor_tensor(o, a, b, ALU.mult),
                    waits=dlz.w() + [("dsz0", 1)] + dprev, inc=dcz.inc())
            nszw = nzc[0].w() + nzc[1].w()
            dszw = dcz.w()

            for li in range(nblk):
                blk = blk0 + li
                dma("sync", u8g, d["g8"][blk], u8L[0].inc(16),
                    waits=dqh[0].w() + dqlw[0].w())
                dma("sync", u8u, d["up8"][blk], u8L[1].inc(16),
                    waits=dqh[1].w() + dqlw[1].w())
                for gp in range(4):
                    src = stg_g2[gp, blk].flatten().unsqueeze(0)
                    src = src.broadcast_to([32, 16 * 256])
                    dma("gpsimd", repg[32 * gp:32 * gp + 32, :], src, repL[0].inc(16),
                        waits=(dqh[0].w() + dql[0].w() + stg_done) if gp == 0 else ())
                for gp in range(4):
                    src = stg_u2[gp, blk].flatten().unsqueeze(0)
                    src = src.broadcast_to([32, 16 * 256])
                    dma("gpsimd", repu[32 * gp:32 * gp + 32, :], src, repL[1].inc(16),
                        waits=(dqh[1].w() + dql[1].w() + stg_done) if gp == 0 else ())
                for wi in range(2):
                    u8b = u8g if wi == 0 else u8u
                    rep = repg if wi == 0 else repu
                    nsz = nszg if wi == 0 else nszu
                    bk_base = 0 if wi == 0 else 4
                    for ha in range(2):
                        db = ha
                        dqw = [(f"deqF{db}", p.cur(f"deqF{db}")),
                               (u8L[wi].name, p.cur(u8L[wi].name)),
                               (repL[wi].name, p.cur(repL[wi].name))]
                        for jj in range(8):
                            jc = ha * 8 + jj
                            u8t = u8b[:, jc * 256:(jc + 1) * 256]
                            rpt = rep[:, jc * 256:(jc + 1) * 256]
                            dequant("gpsimd", u8t, rpt,
                                    deqh[db][:, (2 * jj) * 256:(2 * jj + 1) * 256],
                                    deqh[db][:, (2 * jj + 1) * 256:(2 * jj + 2) * 256],
                                    hn_gu, ln_gu, dqh[wi], dqlw[wi], dql[wi],
                                    dqw if jj == 0 else ())
                        dqr = dqh[wi].w() + dql[wi].w()
                        for nib in range(2):
                            for half in range(2):
                                bk = banks[bk_base + 2 * nib + half]
                                for jj in range(8):
                                    jc = ha * 8 + jj
                                    for pl in range(2):
                                        xchunk = jc if pl == 0 else 16 + jc
                                        lhs = deqh[db][:, (2 * jj + pl) * 256 + nib * 128:
                                                       (2 * jj + pl) * 256 + nib * 128 + 128]
                                        rhs = xn_own[:, xchunk * TOWN + half * 512:
                                                     xchunk * TOWN + half * 512 + 512]
                                        first = (ha == 0 and jj == 0 and pl == 0)
                                        last_of_half = (nib == 1 and half == 1 and
                                                        jj == 7 and pl == 1)
                                        mmop(bk.ap, lhs, rhs, first, False,
                                             waits=(dqr + bk.ww() + nszw + xg_done)
                                             if (jj == 0 and pl == 0) else [],
                                             inc=(f"deqF{db}", 1) if last_of_half else None)
                                if ha == 1:
                                    lloc = (2 * li + nib) * 128
                                    mmop(bk.ap, nsz[0:64, lloc:lloc + 128],
                                         xgT[0:64, half * 512:half * 512 + 512],
                                         False, True, inc=bk.wi())
                for nib in range(2):
                    cl = 2 * li + nib
                    for half in range(2):
                        gb = banks[2 * nib + half]
                        ub = banks[4 + 2 * nib + half]
                        act(t1b[half], gb.ap, AF.Silu,
                            waits=gb.rw() + t1_free[half], inc=gb.ri())
                        vop("vector", lambda e,
                            o=hact[:, cl * TOWN + half * 512: cl * TOWN + half * 512 + 512],
                            a=t1b[half], b=ub.ap: e.tensor_tensor(o, a, b, ALU.mult),
                            waits=ub.rw() + gb.rdone() + dprev, inc=ub.ri())
                        t1_free[half] = ub.rdone()
            hact_done = []
            for bb in banks[4:8]:
                hact_done += bb.rdone()

            for half in range(2):
                bk = banks[half]
                for cl in range(nch):
                    off = 28 - 4 * (cl // 2)
                    mmop(bk.ap[0:32, :], bbuf_s[:, off:off + 32],
                         hact[:, cl * TOWN + half * 512: cl * TOWN + half * 512 + 512],
                         cl == 0, cl == nch - 1,
                         waits=(bk.ww() + hact_done + [("c_bb", 16)])
                         if cl == 0 else [],
                         inc=bk.wi() if cl == nch - 1 else None)
                act(xdT[0:32, half * 512:half * 512 + 512], bk.ap[0:32, :], AF.Copy,
                    waits=bk.rw() + dprev, inc=bk.ri())
            xdw = banks[0].rdone() + banks[1].rdone()

            for hg in range(8):
                dq = hg % 2
                for gp in range(4):
                    src = stg_d2[gp, hg, blk0:blk0 + nblk, :].flatten().unsqueeze(0)
                    src = src.broadcast_to([32, nblk * 512])
                    dma("gpsimd", repd[32 * gp:32 * gp + 32, 0:nblk * 512], src,
                        rdL.inc(16),
                        waits=(dqhd.w() + dqld.w() + stg_done) if gp == 0 else ())
                ddqf = (banks[2].rdone() + banks[3].rdone() +
                        banks[0].rdone() + banks[1].rdone())
                srcd = d["d8"].rearrange("b p n -> p b n")[:, blk0:blk0 + nblk,
                                                          hg * 512:(hg + 1) * 512]
                dma("sync", d8all[0][:, 0:nblk * 512], srcd,
                    d8L[0].inc(16), waits=dqhd.w() + dqldw.w())
                for li in range(nblk):
                    w0 = d8L[0].w() + rdL.w() + (ddqf if li == 0 else [])
                    dequant("gpsimd", d8all[0][:, li * 512:(li + 1) * 512],
                            repd[:, li * 512:(li + 1) * 512],
                            ddeq[dq][:, (2 * li) * 512:(2 * li + 1) * 512],
                            ddeq[dq][:, (2 * li + 1) * 512:(2 * li + 2) * 512],
                            hn_d, ln_d, dqhd, dqldw, dqld, w0)
                dqr2 = dqhd.w() + dqld.w()
                for ti in range(8):
                    bk = banks[(2, 3, 0, 1)[ti % 4]]
                    for li in range(nblk):
                        for nib in range(2):
                            cl = 2 * li + nib
                            lhs = hact[:, cl * TOWN + ti * 128: cl * TOWN + ti * 128 + 128]
                            rhs = ddeq[dq][:, cl * 512:(cl + 1) * 512]
                            first = (li == 0 and nib == 0)
                            mmop(bk.ap, lhs, rhs, first, False,
                                 waits=(bk.ww() + dqr2 + xdw + dszw) if first else [])
                    mmop(bk.ap, xdT[0:32, ti * 128:ti * 128 + 128],
                         dszr[0:32, hg * 512:(hg + 1) * 512],
                         False, True, inc=bk.wi())
                    dsl = dpart[ti * 128:(ti + 1) * 128, hg * 512:(hg + 1) * 512]
                    pb = (hg * 8 + ti) % 2
                    rbw = (ptF[0].w() + ptF[1].w()) \
                        if (hg == 0 and ti == 0 and rnd > 0) else []
                    if rnd == 0:
                        act(ptl[pb], bk.ap, AF.Copy,
                            waits=bk.rw() + ptF[pb].w(), inc=bk.ri())
                        dma("sync", dsl, ptl[pb], ptF[pb].inc(16),
                            waits=bk.rdone())
                    elif rnd < NRND - 1 or not full_ffn:
                        act(ptl[pb], bk.ap, AF.Copy,
                            waits=bk.rw() + ptF[pb].w(), inc=bk.ri())
                        p.op("gpsimd", lambda e, o=dsl, i=ptl[pb]:
                             e.dma_start(out=o, in_=i, accum_op=ALU.add),
                             waits=bk.rdone() + rbw, inc=ptF[pb].inc(16))
                    else:
                        sb = (hg * 8 + ti) % 2
                        dma("sync", pin[pb], dsl, pinL[pb].inc(16),
                            waits=rbw + pin_free[pb])
                        vop("vector", lambda e, o=sscr[sb], i=bk.ap, m=pin[pb]:
                            e.scalar_tensor_tensor(o, i, 1.0, m, ALU.mult, ALU.add),
                            waits=bk.rw() + pinL[pb].w() + gsl[sb].w(),
                            inc=bk.ri())
                        pin_free[pb] = bk.rdone()
                        act(gtl[sb], sscr[sb], AF.Silu,
                            waits=bk.rdone() + gsf[sb].w(), inc=gsl[sb].inc())
                        dma("sync", gbounce[ti * 128:(ti + 1) * 128,
                                            hg * 512:(hg + 1) * 512],
                            gtl[sb], gsf[sb].inc(16), waits=gsl[sb].w())
            blk0 += nblk

        abf.pop()
        af32.pop()
        au8.pop()
        p.barrier("ph3")

        avT = abf.alloc(NH * TOWN, tag="avT")
        abf.push()
        af32.push()
        kT4 = abf.alloc(SEQ, tag="kT4")
        V4 = abf.alloc(SEQ, tag="V4")
        wqb = [abf.alloc(HC * DH), abf.alloc(HC * DH)]
        qh = [abf.alloc(TOWN, tag="qh"), abf.alloc(TOWN)]
        ptb = [[abf.alloc(512) for _ in range(3)], [abf.alloc(512) for _ in range(3)]]
        rrow_b = abf.alloc(512)
        m2_s = [af32.alloc(2432), af32.alloc(2432)]
        vscr = [[af32.alloc(512), af32.alloc(512)], [af32.alloc(512), af32.alloc(512)]]
        rf = af32.alloc(512)

        kvl = C("kvl")
        wqL = [C("wqL0"), C("wqL1")]
        rrb = C("rrb")
        ptbL = [[C("ptbL00"), C("ptbL01"), C("ptbL02")],
                [C("ptbL10"), C("ptbL11"), C("ptbL12")]]

        dma("sync", kT4, kv_dram[:, 0:SEQ], kvl.inc(16), waits=[("kvb", 32)])
        dma("sync", V4, kv_dram[:, SEQ:2 * SEQ], kvl.inc(16))
        dma("sync", m2_s[0], d["m2q0"], kvl.inc(16))
        dma("sync", m2_s[1], d["m2q1"], kvl.inc(16))
        kvw = kvl.w()

        KC_OF_QB = (8, 16)
        ptb_free = [[[], [], []], [[], [], []]]
        vs_free = [[[], []], [[], []]]
        rf_free = []
        rr_free = []
        for hp in range(NH // 2 if KBISECT >= 4 else 0):
            heads = (2 * hp, 2 * hp + 1)
            for s, h in enumerate(heads):
                wb = h % 2
                dma("gpsimd", wqb[wb], d["wq"][h], wqL[wb].inc(16),
                    waits=(banks[0].wdone() + banks[1].wdone()) if h >= 2 else ())
            sc_bk = [[banks[2], banks[6]], [banks[3], banks[7]]]
            av_bk = [banks[4], banks[5]]
            sum_bk = [banks[0], banks[1]]
            q_ready = []
            for s, h in enumerate(heads):
                wb = h % 2
                qhf = (sc_bk[s][0].wdone() + sc_bk[s][1].wdone()) if hp else []
                for half in range(2):
                    bk = banks[half]
                    for hc in range(HC):
                        mmop(bk.ap, wqb[wb][:, hc * DH:(hc + 1) * DH],
                             xn_own[:, hc * TOWN + half * 512: hc * TOWN + half * 512 + 512],
                             hc == 0, hc == HC - 1,
                             waits=(bk.ww() + wqL[wb].w()) if hc == 0 else [],
                             inc=bk.wi() if hc == HC - 1 else None)
                    act(qh[s][:, half * 512:half * 512 + 512], bk.ap, AF.Copy,
                        scale=sd / slopesf[h], waits=bk.rw() + qhf, inc=bk.ri())
                q_ready = banks[0].rdone() + banks[1].rdone()
            for qb in range(2):
                nkc = KC_OF_QB[qb]

                def emit_scores(kc):
                    for s, h in enumerate(heads):
                        vb = kc % 2
                        bk = sc_bk[s][vb]
                        mmop(bk.ap, kT4[:, kc * 128:(kc + 1) * 128],
                             qh[s][:, qb * 512:qb * 512 + 512], True, True,
                             waits=bk.ww() + q_ready + kvw, inc=bk.wi())

                emit_scores(0)
                for kc in range(nkc):
                    if kc + 1 < nkc:
                        emit_scores(kc + 1)
                    for s, h in enumerate(heads):
                        vb = kc % 2
                        tb = kc % 3
                        bk = sc_bk[s][vb]
                        m2v = m2_s[qb][:, 1920 - kc * 128: 1920 - kc * 128 + 512]
                        vop("vector", lambda e, o=vscr[s][vb], i=bk.ap, m=m2v:
                            e.scalar_tensor_tensor(o, i, 1.0, m, ALU.mult, ALU.add),
                            waits=bk.rw() + vs_free[s][vb], inc=bk.ri())
                        act(ptb[s][tb], vscr[s][vb], AF.Exp, scale=slopesf[h],
                            waits=bk.rdone() + ptb_free[s][tb],
                            inc=ptbL[s][tb].inc())
                        vs_free[s][vb] = ptbL[s][tb].w()
                        ptw = ptbL[s][tb].w()
                        mmop(sum_bk[s].ap[0:1, :], onescol, ptb[s][tb],
                             kc == 0, kc == nkc - 1,
                             waits=ptw + (sum_bk[s].ww() if kc == 0 else []),
                             inc=sum_bk[s].wi() if kc == nkc - 1 else None)
                        lastmm = (kc == nkc - 1)
                        mmop(av_bk[s].ap, V4[:, kc * 128:(kc + 1) * 128], ptb[s][tb],
                             kc == 0, lastmm,
                             waits=ptw + (av_bk[s].ww() if kc == 0 else []),
                             inc=av_bk[s].wi() if lastmm else (f"ptbF{s}{tb}", 1))
                        if lastmm:
                            ptb_free[s][tb] = [(f"{av_bk[s].name}W", av_bk[s].w)]
                        else:
                            ptb_free[s][tb] = [(f"ptbF{s}{tb}", p.cur(f"ptbF{s}{tb}"))]
                for s, h in enumerate(heads):
                    vop("vector", lambda e, o=rf[0:1, :], i=sum_bk[s].ap[0:1, :]:
                        e.reciprocal(o, i),
                        waits=sum_bk[s].rw() + rr_free, inc=sum_bk[s].ri())
                    vop("vector", lambda e, o=rrow_b[0:1, :], i=rf[0:1, :]:
                        e.tensor_copy(o, i), inc=rrb.inc())
                    rep_bk = sc_bk[s][0]
                    mmop(rep_bk.ap, ones1, rrow_b[0:1, :], True, True,
                         waits=rep_bk.ww() + rrb.w(), inc=rep_bk.wi())
                    rr_free = [(f"{rep_bk.name}W", rep_bk.w)]
                    act(rf, rep_bk.ap, AF.Copy,
                        waits=rep_bk.rw() + rf_free, inc=rep_bk.ri())
                    vop("vector", lambda e,
                        o=avT[:, h * TOWN + qb * 512: h * TOWN + qb * 512 + 512],
                        a=av_bk[s].ap, b=rf: e.tensor_tensor(o, a, b, ALU.mult),
                        waits=av_bk[s].rw() + rep_bk.rdone(), inc=av_bk[s].ri())
                    rf_free = [(f"{av_bk[s].name}R", av_bk[s].r)]
        av_done = banks[4].rdone() + banks[5].rdone()

        abf.pop()
        af32.pop()
        p.barrier("ph4")

        abf.push()
        af32.push()
        wob = abf.alloc(HC * 512)
        gio = [abf.alloc(512), abf.alloc(512)]
        xio = [af32.alloc(512), af32.alloc(512)]
        oscr = [af32.alloc(512), af32.alloc(512)]
        oout = [af32.alloc(512), af32.alloc(512)]
        woL = C("woL")
        giL = [C("giL0"), C("giL1")]
        xiL = [C("xiL0"), C("xiL1")]
        ooL = [C("ooL0"), C("ooL1")]
        ooF = [C("ooF0"), C("ooF1")]
        g_all = (gsf[0].w() + gsf[1].w()) if full_ffn else []
        gi_free = [[], []]
        os_free = [[], []]

        if KBISECT < 5:
            p.op("gpsimd", lambda e, o=oout[0]: e.memset(o, 0.0), inc=("stub", 1))
            for hg in range(8):
                for ti in range(8):
                    dma("sync", out_d[ti * 128:(ti + 1) * 128, hg * 512:(hg + 1) * 512],
                        oout[0], ("stubo", 16), waits=[("stub", 1)])
            p.op("sync", lambda e: e.nop(), waits=[("stubo", p.cur("stubo"))])
        for hg in range(8 if KBISECT >= 5 else 0):
            dma("gpsimd", wob, d["wo"][hg], woL.inc(16),
                waits=(banks[0].wdone() + banks[1].wdone() +
                       banks[2].wdone() + banks[3].wdone()) if hg else ())
            for ti in range(8):
                it = hg * 8 + ti
                gb2 = it % 2
                dma("sync", gio[gb2], gbounce[ti * 128:(ti + 1) * 128, hg * 512:(hg + 1) * 512],
                    giL[gb2].inc(16), waits=gi_free[gb2] + g_all)
                xb = it % 2
                dma("sync", xio[xb], d["xown"][ti * 128:(ti + 1) * 128, hg * 512:(hg + 1) * 512],
                    xiL[xb].inc(16), waits=ooL[xb].w())
                bk = banks[ti % 4]
                for hh in range(NH):
                    mmop(bk.ap, avT[:, hh * TOWN + ti * 128: hh * TOWN + ti * 128 + 128],
                         wob[:, hh * 512:(hh + 1) * 512], hh == 0, hh == NH - 1,
                         waits=(bk.ww() + woL.w() + av_done) if hh == 0 else [],
                         inc=bk.wi() if hh == NH - 1 else None)
                ob = it % 2
                vop("vector", lambda e, o=oscr[ob], a=bk.ap, b=gio[gb2]:
                    e.tensor_tensor(o, a, b, ALU.mult),
                    waits=bk.rw() + giL[gb2].w() + os_free[ob], inc=bk.ri())
                gi_free[gb2] = bk.rdone()
                vop("vector", lambda e, o=oout[ob], a=oscr[ob], b=xio[xb]:
                    e.tensor_tensor(o, a, b, ALU.add),
                    waits=bk.rdone() + xiL[xb].w() + ooF[ob].w(), inc=ooL[ob].inc())
                os_free[ob] = ooL[ob].w()
                dma("sync", out_d[ti * 128:(ti + 1) * 128, hg * 512:(hg + 1) * 512],
                    oout[ob], ooF[ob].inc(16), waits=ooL[ob].w())
        if KB_DBG:
            dma("sync", dbg_xn, xn_own, ("dbgs", 16))
            dma("sync", dbg_av, avT, ("dbgs", 16))
            p.op("sync", lambda e: e.nop(), waits=[("dbgs", p.cur("dbgs"))])
        if KBISECT >= 5:
            p.op("sync", lambda e: e.nop(), waits=ooF[0].w() + ooF[1].w())

        with contextlib.ExitStack() as ss:
            sems = {n: ss.enter_context(nc.semaphore(n)) for n in p.plan}
            with nc.Block() as blk:
                @blk.tensor
                def _(e):
                    p.emit("tensor", e, sems)

                @blk.vector
                def _(e):
                    p.emit("vector", e, sems)

                @blk.scalar
                def _(e):
                    p.emit("scalar", e, sems)

                @blk.gpsimd
                def _(e):
                    p.emit("gpsimd", e, sems)

                @blk.sync
                def _(e):
                    p.emit("sync", e, sems)
        pstack.close()
    return nc


def kernel(**inputs):
    maps, own_blocks = host_prep(inputs)
    nc = build_program()
    res = run_bass_kernel_spmd(nc, maps, core_ids=list(range(NCORES)))
    out = np.zeros((4, SEQ, H), dtype=np.float32)
    for core in range(NCORES):
        seq = core // 2
        blocks = own_blocks[core % 2]
        o = np.asarray(res.results[core]["out"])
        out[seq, blocks[0][0]:blocks[0][1], :] = o[0:512, :]
        out[seq, blocks[1][0]:blocks[1][1], :] = o[512:1024, :]
    return out


# version 18
# speedup vs baseline: 1.2258x; 1.0771x over previous
import contextlib
import os
import numpy as np
import ml_dtypes
import concourse.bass as bass
import concourse.mybir as mybir
from concourse.bass_utils import run_bass_kernel_spmd

F32 = mybir.dt.float32
BF16 = mybir.dt.bfloat16
U8 = mybir.dt.uint8
AF = mybir.ActivationFunctionType
ALU = mybir.AluOpType
BFD = ml_dtypes.bfloat16

H = 4096
SEQ = 2048
TOWN = 1024
NH = 32
DH = 128
II = 11008
NB = 43
HC = 32
NCORES = 8
ROUND_BLOCKS = [4] * 10 + [3]
NRND = len(ROUND_BLOCKS)
MASK_NEG = -1.0e9
EPS = 1e-5
KBISECT = int(os.environ.get('KBISECT', '5'))
KB_ROUNDS = int(os.environ.get('KB_ROUNDS', '99'))
KB_DBG = int(os.environ.get('KB_DBG', '0'))


def _slopes():
    start = 2.0 ** (-(2.0 ** (-(np.log2(NH) - 3))))
    return np.array([start ** (i + 1) for i in range(NH)], dtype=np.float32)


def _plane_perm():
    return np.concatenate([np.arange(0, H, 2), np.arange(1, H, 2)])


def _ic_order():
    cols = []
    for b in range(NB):
        cols.append(np.arange(256 * b, 256 * b + 256, 2))
        cols.append(np.arange(256 * b + 1, 256 * b + 256, 2))
    return np.concatenate(cols)


def _u8_of_i32(a):
    return np.ascontiguousarray(
        a.view(np.uint8).reshape(a.shape[0], a.shape[1], 4)[:, :, 0])


OFFS = {}
ENGINES = ("tensor", "vector", "scalar", "gpsimd", "sync")


class Prog:
    def __init__(self):
        self.ops = {e: [] for e in ENGINES}
        self.plan = {}
        self.ensured = {}

    def note(self, s):
        if s not in self.plan:
            self.plan[s] = 0

    def cur(self, s):
        self.note(s)
        return self.plan[s]

    def op(self, eng, fn, waits=(), inc=None):
        rw = []
        for (s, v) in waits:
            self.note(s)
            assert v <= self.plan[s], f"wait on future {s}: {v} > {self.plan[s]}"
            if v <= 0:
                continue
            k = (eng, s)
            if self.ensured.get(k, 0) >= v:
                continue
            self.ensured[k] = v
            rw.append((s, v))
        if inc is not None:
            s, amt = inc
            self.note(s)
            self.plan[s] += amt
        self.ops[eng].append((rw, fn, inc))

    def barrier(self, name):
        for e in ENGINES:
            self.op(e, lambda en: en.nop(), inc=(name, 1))
        tgt = self.cur(name)
        for e in ENGINES:
            self.op(e, lambda en: en.nop(), waits=[(name, tgt)])

    def emit(self, eng, e, sems):
        for (waits, fn, inc) in self.ops[eng]:
            for (s, v) in waits:
                e.wait_ge(sems[s], v)
            ins = fn(e)
            if inc is not None:
                ins.then_inc(sems[inc[0]], inc[1])


class Arena:
    def __init__(self, ap, total, name):
        self.ap = ap
        self.total = total
        self.off = 0
        self.marks = []
        self.name = name
        self.peak = 0

    def alloc(self, n, align=128, tag=None):
        o = (self.off + align - 1) // align * align
        assert o + n <= self.total, f"{self.name} overflow: {o}+{n} > {self.total}"
        self.off = o + n
        self.peak = max(self.peak, self.off)
        if tag is not None:
            OFFS[tag] = (self.name, o, n)
        return self.ap[:, o:o + n]

    def push(self):
        self.marks.append(self.off)

    def pop(self):
        self.off = self.marks.pop()


class Ctr:
    def __init__(self, prog, name):
        self.p, self.name = prog, name

    def inc(self, amt=1):
        return (self.name, amt)

    def w(self):
        return [(self.name, self.p.cur(self.name))]


class Bank:
    def __init__(self, name, ap):
        self.name, self.ap = name, ap
        self.w = 0
        self.r = 0

    def ww(self):
        return [] if self.r == 0 else [(f"{self.name}R", self.r)]

    def wi(self):
        self.w += 1
        return (f"{self.name}W", 1)

    def rw(self):
        return [(f"{self.name}W", self.w)]

    def ri(self):
        self.r += 1
        return (f"{self.name}R", 1)

    def rdone(self):
        return [] if self.r == 0 else [(f"{self.name}R", self.r)]

    def wdone(self):
        return [] if self.w == 0 else [(f"{self.name}W", self.w)]


def host_prep(inp):
    pp = _plane_perm()
    ico = _ic_order()
    x = np.asarray(inp["L0_attn_hidden_states"], dtype=np.float32)
    Wq = np.asarray(inp["L0_attn_Wq"], dtype=np.float32)
    Wkv = np.asarray(inp["L0_attn_Wkv"], dtype=np.float32)
    Wo = np.asarray(inp["L0_attn_Wo"], dtype=np.float32)

    wq_t = np.ascontiguousarray(Wq[pp, :].reshape(HC, 128, NH, DH).transpose(2, 1, 0, 3))
    wkv_t = np.ascontiguousarray(Wkv[pp, :].reshape(HC, 128, 256).transpose(1, 0, 2))
    wo_t = np.ascontiguousarray(Wo.reshape(HC, 128, 8, 512).transpose(2, 1, 0, 3))

    def prep_gu(wname):
        u8 = _u8_of_i32(np.asarray(inp[wname]))
        u8r = np.ascontiguousarray(u8.T)[:, ico]
        return np.ascontiguousarray(u8r.reshape(16, 128, NB, 256).transpose(2, 1, 0, 3))

    g8 = prep_gu("L0_ffn_gate_Wq")
    up8 = prep_gu("L0_ffn_up_Wq")
    d8 = _u8_of_i32(np.asarray(inp["L0_ffn_down_Wq"]))
    d8t = np.ascontiguousarray(d8.T.reshape(NB, 128, H))

    sT_g = np.ascontiguousarray(np.asarray(inp["L0_ffn_gate_scale"], np.float32).T[:, ico])
    zT_g = np.ascontiguousarray(np.asarray(inp["L0_ffn_gate_zero"], np.float32).T[:, ico])
    sT_u = np.ascontiguousarray(np.asarray(inp["L0_ffn_up_scale"], np.float32).T[:, ico])
    zT_u = np.ascontiguousarray(np.asarray(inp["L0_ffn_up_zero"], np.float32).T[:, ico])
    sT_d = np.ascontiguousarray(np.asarray(inp["L0_ffn_down_scale"], np.float32).T)
    zT_d = np.ascontiguousarray(np.asarray(inp["L0_ffn_down_zero"], np.float32).T)

    B1 = np.zeros((H, 64), dtype=BFD)
    B1[np.arange(H), pp // 64] = -1.0
    b1_t = np.ascontiguousarray(B1.reshape(HC, 128, 64).transpose(1, 0, 2))

    Bb = np.zeros((128, 60), dtype=BFD)
    for pr in range(128):
        Bb[pr, 28 + pr // 32] = -1.0

    shared = dict(wq=wq_t, wkv=wkv_t, wo=wo_t, g8=g8, up8=up8, d8=d8t,
                  sT_g=sT_g, zT_g=zT_g, sT_u=sT_u, zT_u=zT_u, sT_d=sT_d,
                  zT_d=zT_d, b1=b1_t, bbuf=Bb)

    own_blocks = [[(0, 512), (1536, 2048)], [(512, 1024), (1024, 1536)]]
    maps = []
    for core in range(NCORES):
        seq = core // 2
        blocks = own_blocks[core % 2]
        own_idx = np.concatenate([np.arange(a, b) for (a, b) in blocks])
        xT = np.ascontiguousarray(x[seq].T[pp, :])
        xTown = np.ascontiguousarray(xT[:, own_idx])
        xown = np.ascontiguousarray(x[seq][own_idx, :])
        m2 = []
        for (a, _b) in blocks:
            c = np.arange(2432)
            pr = np.arange(128)
            dd = a + c[None, :] - 1920 - pr[:, None]
            m2.append(np.where(dd >= 0, -dd, MASK_NEG).astype(np.float32))
        m = dict(shared)
        m.update(xT=xT, xTown=xTown, xown=xown, m2q0=m2[0], m2q1=m2[1])
        maps.append(m)
    return maps, own_blocks


def build_program():
    slopesf = [float(s) for s in _slopes()]
    sd = float(1.0 / np.sqrt(DH))
    nc = bass.Bass("TRN2", target_bir_lowering=False)

    d = {}
    for name, shape, dt in [
        ("xT", [H, SEQ], F32), ("xTown", [H, TOWN], F32), ("xown", [TOWN, H], F32),
        ("m2q0", [128, 2432], F32), ("m2q1", [128, 2432], F32),
        ("wq", [NH, 128, HC, DH], F32), ("wkv", [128, HC, 256], F32),
        ("wo", [8, 128, HC, 512], F32),
        ("g8", [NB, 128, 16, 256], U8), ("up8", [NB, 128, 16, 256], U8),
        ("d8", [NB, 128, H], U8),
        ("sT_g", [64, II], F32), ("zT_g", [64, II], F32),
        ("sT_u", [64, II], F32), ("zT_u", [64, II], F32),
        ("sT_d", [172, H], F32), ("zT_d", [172, H], F32),
        ("b1", [128, HC, 64], BF16), ("bbuf", [128, 60], BF16),
    ]:
        d[name] = nc.dram_tensor(name, shape, dt, kind="ExternalInput").ap()
    out_d = nc.dram_tensor("out", [TOWN, H], F32, kind="ExternalOutput").ap()
    kdbg = "ExternalOutput" if KB_DBG else "Internal"
    dpart = nc.dram_tensor("dpart", [TOWN, H], F32, kind=kdbg).ap()
    gbounce = nc.dram_tensor("gbounce", [TOWN, H], BF16, kind=kdbg).ap()
    kv_dram = nc.dram_tensor("kv_dram", [128, 2 * SEQ], BF16, kind=kdbg).ap()
    stg_g = nc.dram_tensor("stg_g", [64, II], BF16, kind="Internal").ap()
    stg_u = nc.dram_tensor("stg_u", [64, II], BF16, kind="Internal").ap()
    stg_d = nc.dram_tensor("stg_d", [172, H], BF16, kind="Internal").ap()
    stg_g2 = nc.dram_tensor("stg_g2", [4, NB, 16, 256], BF16, kind="Internal").ap()
    stg_u2 = nc.dram_tensor("stg_u2", [4, NB, 16, 256], BF16, kind="Internal").ap()
    stg_d2 = nc.dram_tensor("stg_d2", [4, 8, NB, 512], BF16, kind="Internal").ap()
    if KB_DBG:
        dbg_xn = nc.dram_tensor("dbg_xn", [128, HC * TOWN], BF16, kind="ExternalOutput").ap()
        dbg_av = nc.dram_tensor("dbg_av", [128, NH * TOWN], BF16, kind="ExternalOutput").ap()

    p = Prog()

    BF_COLS = 84992
    F32_COLS = 7424
    U8_COLS = 12288

    with (
        nc.sbuf_tensor("a_bf", [128, BF_COLS], BF16) as a_bf_t,
        nc.sbuf_tensor("a_f32", [128, F32_COLS], F32) as a_f32_t,
        nc.sbuf_tensor("a_u8", [128, U8_COLS], U8) as a_u8_t,
        nc.sbuf_tensor("ones1", [1, 128], BF16) as ones1_t,
        nc.sbuf_tensor("onescol", [128, 1], BF16) as onescol_t,
        nc.sbuf_tensor("epsc", [128, 1], F32) as epsc_t,
    ):
        abf = Arena(a_bf_t[:], BF_COLS, "bf16")
        af32 = Arena(a_f32_t[:], F32_COLS, "f32")
        au8 = Arena(a_u8_t[:], U8_COLS, "u8")
        ones1 = ones1_t[:]
        onescol = onescol_t[:]
        epsc = epsc_t[:]

        pstack = contextlib.ExitStack()
        banks = []
        for i in range(8):
            t = pstack.enter_context(nc.psum_tensor(f"pb{i}", [128, 512], F32))
            banks.append(Bank(f"b{i}", t[:]))

        def dma(eng, out, in_, inc, waits=()):
            p.op(eng, lambda e, o=out, i=in_: e.dma_start(out=o, in_=i),
                 waits=waits, inc=inc)

        def mmop(bank_ap, lhsT, rhs, start, stop, waits=(), inc=None):
            p.op("tensor", lambda e, o=bank_ap, l=lhsT, r=rhs, s1=start, s2=stop:
                 e.matmul(o, l, r, start=s1, stop=s2), waits=waits, inc=inc)

        def act(out, in_, func, bias=None, scale=None, waits=(), inc=None):
            kw = {}
            if bias is not None:
                kw["bias"] = bias
            if scale is not None:
                kw["scale"] = scale
            p.op("scalar", lambda e, o=out, i=in_, f=func, k=dict(kw):
                 e.activation(o, i, f, **k), waits=waits, inc=inc)

        def vop(eng, fn, waits=(), inc=None):
            p.op(eng, fn, waits=waits, inc=inc)

        def C(name):
            return Ctr(p, name)

        bbuf_s = abf.alloc(64)
        xn_own = abf.alloc(HC * TOWN, tag="xn_own")
        xgT = abf.alloc(TOWN, tag="xgT")

        dma("sync", bbuf_s[:, 0:60], d["bbuf"], ("c_bb", 16))
        abf.push()
        stgb = abf.alloc(II)
        stgc = C("stgc")
        stgo = C("stgo")
        stgo_h2w = []
        for snm, ddr, dst, rows in (("sg", d["sT_g"], stg_g, 64),
                                    ("su", d["sT_u"], stg_u, 64),
                                    ("sd", d["sT_d"], stg_d, 172)):
            if rows == 64:
                dma("gpsimd", stgb[0:64, 0:II], ddr, stgc.inc(16), waits=stgo.w())
                dma("sync", dst, stgb[0:64, 0:II], stgo.inc(16), waits=stgc.w())
            else:
                dma("gpsimd", stgb[0:128, 0:H], ddr[0:128, :], stgc.inc(16),
                    waits=stgo.w())
                dma("gpsimd", stgb[0:44, H:2 * H], ddr[128:172, :], stgc.inc(16))
                dma("sync", dst[0:128, :], stgb[0:128, 0:H], stgo.inc(16),
                    waits=stgc.w())
                dma("sync", dst[128:172, :], stgb[0:44, H:2 * H], stgo.inc(16))
        h2t = abf.alloc(4096)
        h2c = C("h2c")
        for gp in range(4):
            for wsrc, wdst, pat in (
                    (stg_g, stg_g2, "gu"), (stg_u, stg_u2, "gu"), (stg_d, stg_d2, "d")):
                if pat == "gu":
                    src2 = wsrc.rearrange("(jc g) (blk i) -> g blk jc i", g=4, i=256)[gp]
                    dst2 = wdst[gp]
                else:
                    src2 = wsrc.rearrange("(blk g) (hg i) -> g blk hg i", g=4, i=512)[gp]
                    dst2 = wdst[gp].rearrange("hg blk i -> blk hg i")
                dma("gpsimd", h2t[0:NB, :], src2, h2c.inc(16),
                    waits=stgo.w() + stgo_h2w)
                dma("sync", dst2, h2t[0:NB, :], stgo.inc(16), waits=h2c.w())
                stgo_h2w = stgo.w()
        stg_done = stgo.w()
        p.barrier("ph0")
        abf.pop()
        p.op("gpsimd", lambda e: e.memset(ones1, 1.0), inc=("c_o1", 1))
        p.op("gpsimd", lambda e, o=epsc: e.memset(o, EPS), inc=("c_ep", 1))
        p.op("gpsimd", lambda e: e.memset(onescol, 1.0), inc=("c_oc", 1))

        abf.push()
        af32.push()
        wkv_s = abf.alloc(HC * 256)
        b1_s = abf.alloc(HC * 64)
        kT = abf.alloc(SEQ, tag="kT")
        V = abf.alloc(SEQ, tag="V")
        xnq = abf.alloc(HC * 512, tag="xnq")
        sqb = [abf.alloc(512), abf.alloc(512)]
        mean_b = abf.alloc(512)
        rstd_b = abf.alloc(512)
        m_rep = abf.alloc(512)
        r_rep = abf.alloc(512)
        xf = [af32.alloc(512) for _ in range(3)]
        xf_free = [[], [], []]
        strow = af32.alloc(512)
        msq = af32.alloc(512)
        varr = af32.alloc(512)
        lnv = af32.alloc(512)
        rsr = af32.alloc(512)

        dma("gpsimd", wkv_s, d["wkv"], ("c_kv", 16), waits=stg_done)
        dma("sync", b1_s, d["b1"], ("c_b1", 16), waits=stg_done)

        lnx = C("lnx")
        sqc = C("sqc")
        lnr = C("lnr")
        lnmb = C("lnmb")
        lnrb = C("lnrb")
        nrm = C("nrm")

        def ln_pass(xdram, cs, xn_ap, xcol0, xstride, xw=()):
            bkx, bkq = banks[0], banks[1]
            prev_b1w = bkq.wdone()
            xbc_at = []
            for hc in range(HC):
                b3 = hc % 3
                dma("sync", xf[b3], xdram[hc * 128:(hc + 1) * 128, cs:cs + 512],
                    (f"lnx{b3}", 16), waits=list(xw) + xf_free[b3])
                xc = xn_ap[:, xcol0 + hc * xstride: xcol0 + hc * xstride + 512]
                act(xc, xf[b3], AF.Copy,
                    waits=[(f"lnx{b3}", p.cur(f"lnx{b3}"))] + prev_b1w + list(xw),
                    inc=("xbc", 1))
                xf_free[b3] = [("xbc", p.cur("xbc"))]
                xbc_at.append(p.cur("xbc"))
            for hc in range(HC):
                b = hc % 2
                xc = xn_ap[:, xcol0 + hc * xstride: xcol0 + hc * xstride + 512]
                sqfree = [(f"sqF{b}", p.cur(f"sqF{b}"))]
                act(sqb[b], xc, AF.Square,
                    waits=sqfree, inc=sqc.inc())
                mmop(bkx.ap[0:1, :], onescol, xc, hc == 0, hc == HC - 1,
                     waits=(bkx.ww() + [("c_oc", 1)] if hc == 0 else []) +
                     [("xbc", xbc_at[hc])],
                     inc=bkx.wi() if hc == HC - 1 else None)
                mi = (f"sqF{b}", 1) if hc != HC - 1 else bkq.wi()
                mmop(bkq.ap[0:1, :], onescol, sqb[b], hc == 0, hc == HC - 1,
                     waits=sqc.w() + (bkq.ww() if hc == 0 else []), inc=mi)
            vop("vector", lambda e, o=strow[0:1, :], i=bkx.ap[0:1, :]:
                e.tensor_scalar(o, i, 1.0 / H, None, ALU.mult),
                waits=bkx.rw(), inc=bkx.ri())
            vop("vector", lambda e, o=msq[0:1, :], i=strow[0:1, :]:
                e.tensor_tensor(o, i, i, ALU.mult))
            vop("vector", lambda e, o=varr[0:1, :], i=bkq.ap[0:1, :], m=msq[0:1, :]:
                e.scalar_tensor_tensor(o, i, 1.0 / H, m, ALU.mult, ALU.subtract),
                waits=bkq.rw(), inc=bkq.ri())
            act(lnv[0:1, :], varr[0:1, :], AF.Ln, bias=epsc[0:1, 0:1],
                waits=bkq.rdone() + [("c_ep", 1)])
            act(rsr[0:1, :], lnv[0:1, :], AF.Exp, scale=-0.5, inc=lnr.inc())
            vop("vector", lambda e, o=mean_b[0:1, :], i=strow[0:1, :]:
                e.tensor_copy(o, i), inc=lnmb.inc())
            vop("vector", lambda e, o=rstd_b[0:1, :], i=rsr[0:1, :]:
                e.tensor_copy(o, i), waits=lnr.w(), inc=lnrb.inc())
            mmop(bkx.ap, ones1, mean_b[0:1, :], True, True,
                 waits=bkx.ww() + lnmb.w() + [("c_o1", 1)], inc=bkx.wi())
            mmop(bkq.ap, ones1, rstd_b[0:1, :], True, True,
                 waits=bkq.ww() + lnrb.w(), inc=bkq.wi())
            act(m_rep, bkx.ap, AF.Copy, waits=bkx.rw(), inc=bkx.ri())
            act(r_rep, bkq.ap, AF.Copy, waits=bkq.rw(), inc=bkq.ri())
            rpw = bkx.rdone() + bkq.rdone()
            for hc in range(HC):
                xc = xn_ap[:, xcol0 + hc * xstride: xcol0 + hc * xstride + 512]
                vop("vector", lambda e, o=xc, m=m_rep:
                    e.tensor_tensor(o, o, m, ALU.subtract), waits=rpw)
                vop("vector", lambda e, o=xc, r=r_rep:
                    e.tensor_tensor(o, o, r, ALU.mult), inc=nrm.inc())
            return nrm.w()

        for q in range(4):
            xw = (banks[2].rdone() + banks[3].rdone() + banks[4].rdone()) if q else []
            nw = ln_pass(d["xT"], q * 512, xnq, 0, 512, xw=xw)
            bkk = banks[2]
            for hc in range(HC):
                mmop(bkk.ap, wkv_s[:, hc * 256:hc * 256 + 128],
                     xnq[:, hc * 512:(hc + 1) * 512], hc == 0, hc == HC - 1,
                     waits=(bkk.ww() + nw + [("c_kv", 16)]) if hc == 0 else [],
                     inc=bkk.wi() if hc == HC - 1 else None)
            act(kT[:, q * 512:(q + 1) * 512], bkk.ap, AF.Copy,
                waits=bkk.rw(), inc=bkk.ri())
            for kc in range(4 * q, 4 * q + 4):
                bkv = banks[3 + (kc % 2)]
                loc = kc * 128 - q * 512
                for hc in range(HC):
                    mmop(bkv.ap[0:128, 0:128],
                         xnq[:, hc * 512 + loc: hc * 512 + loc + 128],
                         wkv_s[:, hc * 256 + 128: hc * 256 + 256],
                         hc == 0, hc == HC - 1,
                         waits=bkv.ww() if hc == 0 else [],
                         inc=bkv.wi() if hc == HC - 1 else None)
                act(V[:, kc * 128:(kc + 1) * 128], bkv.ap[0:128, 0:128], AF.Copy,
                    waits=bkv.rw(), inc=bkv.ri())
        for q in range(2):
            lo_nw = ln_pass(d["xTown"], q * 512, xn_own, q * 512, TOWN)
        for q in range(2):
            bk = banks[6 + q]
            for hc in range(HC):
                mmop(bk.ap[0:64, :], b1_s[:, hc * 64:(hc + 1) * 64],
                     xn_own[:, hc * TOWN + q * 512: hc * TOWN + q * 512 + 512],
                     hc == 0, hc == HC - 1,
                     waits=(bk.ww() + lo_nw + [("c_b1", 16)]) if hc == 0 else [],
                     inc=bk.wi() if hc == HC - 1 else None)
            act(xgT[0:64, q * 512:(q + 1) * 512], bk.ap[0:64, :], AF.Copy,
                waits=bk.rw(), inc=bk.ri())
        xg_done = banks[6].rdone() + banks[7].rdone()
        dma("sync", kv_dram[:, 0:SEQ], kT, ("kvb", 16), waits=banks[2].rdone())
        dma("sync", kv_dram[:, SEQ:2 * SEQ], V, ("kvb", 16),
            waits=banks[3].rdone() + banks[4].rdone())

        abf.pop()
        af32.pop()
        p.barrier("ph2")

        abf.push()
        af32.push()
        au8.push()
        u8g = au8.alloc(4096)
        u8u = au8.alloc(4096)
        hn_gu = au8.alloc(256)
        ln_gu = au8.alloc(256)
        hn_d = au8.alloc(512)
        ln_d = au8.alloc(512)
        d8all = [au8.alloc(2048)]
        repg = abf.alloc(4096)
        repu = abf.alloc(4096)
        deqh = [abf.alloc(16 * 256), abf.alloc(16 * 256)]
        MAXCH = 2 * max(ROUND_BLOCKS)
        hact = abf.alloc(MAXCH * TOWN, tag="hact")
        t1b = [abf.alloc(512), abf.alloc(512)]
        nszg = abf.alloc(1024, tag="nszg")
        nszu = abf.alloc(1024, tag="nszu")
        dszr = abf.alloc(H, tag="dszr")
        xdT = abf.alloc(TOWN, tag="xdT")
        ddeq = [abf.alloc(8 * 512), abf.alloc(8 * 512)]
        repd = abf.alloc(4 * 512)
        gtl = [abf.alloc(512), abf.alloc(512)]
        ldb = af32.alloc(4096)
        ptl = [af32.alloc(512) for _ in range(3)]
        pin = [af32.alloc(512)]
        sscr = [af32.alloc(512), af32.alloc(512)]

        u8L = [C("u8gL"), C("u8uL")]
        repL = [C("rgL"), C("ruL")]
        dqh = [C("dqh0"), C("dqh1")]
        dqlw = [C("dql0w"), C("dql1w")]
        dql = [C("dql0"), C("dql1")]
        nzc = [C("nzc0"), C("nzc1")]
        nlz = C("nlz")
        dlz = C("dlz")
        dcz = C("dcz")
        d8L = [C("d8L0"), C("d8L1")]
        rdL = C("rdL")
        dqhd = C("dqhd")
        dqldw = C("dqldw")
        dqld = C("dqld")
        ptF = [C("ptF0"), C("ptF1"), C("ptF2")]
        pinL = [C("pinL0")]
        gsl = [C("gsl0"), C("gsl1")]
        gsf = [C("gsf0"), C("gsf1")]

        p.op("gpsimd", lambda e, o=dszr: e.memset(o, 0.0), inc=("dsz0", 1))

        def dequant(lo_eng, u8ap, repap, hi_out, lo_out, hn, ln_,
                    c_h, c_lw, c_l, w0):
            vop("vector", lambda e, o=hn, i=u8ap:
                e.tensor_scalar(o, i, 4, None, ALU.logical_shift_right),
                waits=list(w0))
            vop("vector", lambda e, o=hi_out, i=hn, r=repap:
                e.tensor_tensor(o, i, r, ALU.mult), inc=c_h.inc())
            vop("vector", lambda e, o=ln_, i=u8ap:
                e.tensor_scalar(o, i, 15, None, ALU.bitwise_and),
                waits=c_l.w(), inc=c_lw.inc())
            vop(lo_eng, lambda e, o=lo_out, i=ln_, r=repap:
                e.tensor_tensor(o, i, r, ALU.mult),
                waits=c_lw.w(), inc=c_l.inc())

        rounds = ROUND_BLOCKS[:KB_ROUNDS] if KBISECT >= 3 else []
        full_ffn = (rounds == ROUND_BLOCKS)
        blk0 = 0
        t1_free = [[], []]
        pin_free = [[]]
        for rnd, nblk in enumerate(rounds):
            nch = 2 * nblk
            ispan = nblk * 256
            gd0 = 4 * blk0
            gspan = 4 * nblk
            dprev = (banks[2].rdone() + banks[3].rdone() +
                     banks[0].rdone() + banks[1].rdone()) if rnd else []
            for wi, (sdr, zdr, nsz) in enumerate(((d["sT_g"], d["zT_g"], nszg),
                                                  (d["sT_u"], d["zT_u"], nszu))):
                lw = nzc[0].w() if wi == 1 else (dcz.w() if rnd else [])
                dma("sync", ldb[0:64, 0:ispan], sdr[:, blk0 * 256: blk0 * 256 + ispan],
                    nlz.inc(16), waits=lw)
                dma("sync", ldb[0:64, ispan:2 * ispan],
                    zdr[:, blk0 * 256: blk0 * 256 + ispan], nlz.inc(16))
                nzw = []
                if rnd:
                    for bb in banks:
                        nzw += bb.wdone()
                vop("vector", lambda e, o=nsz[0:64, 0:ispan], a=ldb[0:64, 0:ispan],
                    b=ldb[0:64, ispan:2 * ispan]: e.tensor_tensor(o, a, b, ALU.mult),
                    waits=nlz.w() + nzw, inc=nzc[wi].inc())
            for hh in range(2):
                dma("sync", ldb[0:gspan, 0:2048],
                    d["sT_d"][gd0:gd0 + gspan, hh * 2048:(hh + 1) * 2048],
                    dlz.inc(16), waits=nzc[1].w() if hh == 0 else dcz.w())
                dma("sync", ldb[0:gspan, 2048:4096],
                    d["zT_d"][gd0:gd0 + gspan, hh * 2048:(hh + 1) * 2048],
                    dlz.inc(16))
                vop("vector", lambda e, o=dszr[0:gspan, hh * 2048:(hh + 1) * 2048],
                    a=ldb[0:gspan, 0:2048], b=ldb[0:gspan, 2048:4096]:
                    e.tensor_tensor(o, a, b, ALU.mult),
                    waits=dlz.w() + [("dsz0", 1)] + dprev, inc=dcz.inc())
            nszw = nzc[0].w() + nzc[1].w()
            dszw = dcz.w()

            for li in range(nblk):
                blk = blk0 + li
                dma("sync", u8g, d["g8"][blk], u8L[0].inc(16),
                    waits=dqh[0].w() + dqlw[0].w())
                dma("sync", u8u, d["up8"][blk], u8L[1].inc(16),
                    waits=dqh[1].w() + dqlw[1].w())
                for gp in range(4):
                    src = stg_g2[gp, blk].flatten().unsqueeze(0)
                    src = src.broadcast_to([32, 16 * 256])
                    dma("gpsimd", repg[32 * gp:32 * gp + 32, :], src, repL[0].inc(16),
                        waits=(dqh[0].w() + dql[0].w() + stg_done) if gp == 0 else ())
                for gp in range(4):
                    src = stg_u2[gp, blk].flatten().unsqueeze(0)
                    src = src.broadcast_to([32, 16 * 256])
                    dma("gpsimd", repu[32 * gp:32 * gp + 32, :], src, repL[1].inc(16),
                        waits=(dqh[1].w() + dql[1].w() + stg_done) if gp == 0 else ())
                for wi in range(2):
                    u8b = u8g if wi == 0 else u8u
                    rep = repg if wi == 0 else repu
                    nsz = nszg if wi == 0 else nszu
                    bk_base = 0 if wi == 0 else 4
                    for ha in range(2):
                        db = ha
                        dqw = [(f"deqF{db}", p.cur(f"deqF{db}")),
                               (u8L[wi].name, p.cur(u8L[wi].name)),
                               (repL[wi].name, p.cur(repL[wi].name))]
                        for jj in range(8):
                            jc = ha * 8 + jj
                            u8t = u8b[:, jc * 256:(jc + 1) * 256]
                            rpt = rep[:, jc * 256:(jc + 1) * 256]
                            dequant("vector", u8t, rpt,
                                    deqh[db][:, (2 * jj) * 256:(2 * jj + 1) * 256],
                                    deqh[db][:, (2 * jj + 1) * 256:(2 * jj + 2) * 256],
                                    hn_gu, ln_gu, dqh[wi], dqlw[wi], dql[wi],
                                    dqw if jj == 0 else ())
                        dqr = dqh[wi].w() + dql[wi].w()
                        for nib in range(2):
                            for half in range(2):
                                bk = banks[bk_base + 2 * nib + half]
                                for jj in range(8):
                                    jc = ha * 8 + jj
                                    for pl in range(2):
                                        xchunk = jc if pl == 0 else 16 + jc
                                        lhs = deqh[db][:, (2 * jj + pl) * 256 + nib * 128:
                                                       (2 * jj + pl) * 256 + nib * 128 + 128]
                                        rhs = xn_own[:, xchunk * TOWN + half * 512:
                                                     xchunk * TOWN + half * 512 + 512]
                                        first = (ha == 0 and jj == 0 and pl == 0)
                                        last_of_half = (nib == 1 and half == 1 and
                                                        jj == 7 and pl == 1)
                                        mmop(bk.ap, lhs, rhs, first, False,
                                             waits=(dqr + bk.ww() + nszw + xg_done)
                                             if (jj == 0 and pl == 0) else [],
                                             inc=(f"deqF{db}", 1) if last_of_half else None)
                                if ha == 1:
                                    lloc = (2 * li + nib) * 128
                                    mmop(bk.ap, nsz[0:64, lloc:lloc + 128],
                                         xgT[0:64, half * 512:half * 512 + 512],
                                         False, True, inc=bk.wi())
                for nib in range(2):
                    cl = 2 * li + nib
                    for half in range(2):
                        gb = banks[2 * nib + half]
                        ub = banks[4 + 2 * nib + half]
                        act(t1b[half], gb.ap, AF.Silu,
                            waits=gb.rw() + t1_free[half], inc=gb.ri())
                        vop("vector", lambda e,
                            o=hact[:, cl * TOWN + half * 512: cl * TOWN + half * 512 + 512],
                            a=t1b[half], b=ub.ap: e.tensor_tensor(o, a, b, ALU.mult),
                            waits=ub.rw() + gb.rdone() + dprev, inc=ub.ri())
                        t1_free[half] = ub.rdone()
            hact_done = []
            for bb in banks[4:8]:
                hact_done += bb.rdone()

            for half in range(2):
                bk = banks[half]
                for cl in range(nch):
                    off = 28 - 4 * (cl // 2)
                    mmop(bk.ap[0:32, :], bbuf_s[:, off:off + 32],
                         hact[:, cl * TOWN + half * 512: cl * TOWN + half * 512 + 512],
                         cl == 0, cl == nch - 1,
                         waits=(bk.ww() + hact_done + [("c_bb", 16)])
                         if cl == 0 else [],
                         inc=bk.wi() if cl == nch - 1 else None)
                act(xdT[0:32, half * 512:half * 512 + 512], bk.ap[0:32, :], AF.Copy,
                    waits=bk.rw() + dprev, inc=bk.ri())
            xdw = banks[0].rdone() + banks[1].rdone()

            for hg in range(8):
                dq = hg % 2
                for gp in range(4):
                    src = stg_d2[gp, hg, blk0:blk0 + nblk, :].flatten().unsqueeze(0)
                    src = src.broadcast_to([32, nblk * 512])
                    dma("gpsimd", repd[32 * gp:32 * gp + 32, 0:nblk * 512], src,
                        rdL.inc(16),
                        waits=(dqhd.w() + dqld.w() + stg_done) if gp == 0 else ())
                ddqf = (banks[2].rdone() + banks[3].rdone() +
                        banks[0].rdone() + banks[1].rdone())
                srcd = d["d8"].rearrange("b p n -> p b n")[:, blk0:blk0 + nblk,
                                                          hg * 512:(hg + 1) * 512]
                dma("sync", d8all[0][:, 0:nblk * 512], srcd,
                    d8L[0].inc(16), waits=dqhd.w() + dqldw.w())
                for li in range(nblk):
                    w0 = d8L[0].w() + rdL.w() + (ddqf if li == 0 else [])
                    dequant("vector", d8all[0][:, li * 512:(li + 1) * 512],
                            repd[:, li * 512:(li + 1) * 512],
                            ddeq[dq][:, (2 * li) * 512:(2 * li + 1) * 512],
                            ddeq[dq][:, (2 * li + 1) * 512:(2 * li + 2) * 512],
                            hn_d, ln_d, dqhd, dqldw, dqld, w0)
                dqr2 = dqhd.w() + dqld.w()
                for ti in range(8):
                    bk = banks[(2, 3, 0, 1)[ti % 4]]
                    for li in range(nblk):
                        for nib in range(2):
                            cl = 2 * li + nib
                            lhs = hact[:, cl * TOWN + ti * 128: cl * TOWN + ti * 128 + 128]
                            rhs = ddeq[dq][:, cl * 512:(cl + 1) * 512]
                            first = (li == 0 and nib == 0)
                            mmop(bk.ap, lhs, rhs, first, False,
                                 waits=(bk.ww() + dqr2 + xdw + dszw) if first else [])
                    mmop(bk.ap, xdT[0:32, ti * 128:ti * 128 + 128],
                         dszr[0:32, hg * 512:(hg + 1) * 512],
                         False, True, inc=bk.wi())
                    dsl = dpart[ti * 128:(ti + 1) * 128, hg * 512:(hg + 1) * 512]
                    pb = (hg * 8 + ti) % 3
                    rbw = (ptF[0].w() + ptF[1].w() + ptF[2].w()) \
                        if (hg == 0 and ti == 0 and rnd > 0) else []
                    if rnd == 0:
                        act(ptl[pb], bk.ap, AF.Copy,
                            waits=bk.rw() + ptF[pb].w(), inc=bk.ri())
                        dma("sync", dsl, ptl[pb], ptF[pb].inc(16),
                            waits=bk.rdone())
                    elif rnd < NRND - 1 or not full_ffn:
                        act(ptl[pb], bk.ap, AF.Copy,
                            waits=bk.rw() + ptF[pb].w(), inc=bk.ri())
                        p.op("gpsimd", lambda e, o=dsl, i=ptl[pb]:
                             e.dma_start(out=o, in_=i, accum_op=ALU.add),
                             waits=bk.rdone() + rbw, inc=ptF[pb].inc(16))
                    else:
                        sb = (hg * 8 + ti) % 2
                        dma("sync", pin[0], dsl, pinL[0].inc(16),
                            waits=rbw + pin_free[0])
                        vop("vector", lambda e, o=sscr[sb], i=bk.ap, m=pin[0]:
                            e.scalar_tensor_tensor(o, i, 1.0, m, ALU.mult, ALU.add),
                            waits=bk.rw() + pinL[0].w() + gsl[sb].w(),
                            inc=bk.ri())
                        pin_free[0] = bk.rdone()
                        act(gtl[sb], sscr[sb], AF.Silu,
                            waits=bk.rdone() + gsf[sb].w(), inc=gsl[sb].inc())
                        dma("sync", gbounce[ti * 128:(ti + 1) * 128,
                                            hg * 512:(hg + 1) * 512],
                            gtl[sb], gsf[sb].inc(16), waits=gsl[sb].w())
            blk0 += nblk

        abf.pop()
        af32.pop()
        au8.pop()
        p.barrier("ph3")

        avT = abf.alloc(NH * TOWN, tag="avT")
        abf.push()
        af32.push()
        kT4 = abf.alloc(SEQ, tag="kT4")
        V4 = abf.alloc(SEQ, tag="V4")
        wqb = [abf.alloc(HC * DH), abf.alloc(HC * DH)]
        qh = [abf.alloc(TOWN, tag="qh"), abf.alloc(TOWN)]
        ptb = [[abf.alloc(512) for _ in range(3)], [abf.alloc(512) for _ in range(3)]]
        rrow_b = abf.alloc(512)
        m2_s = [af32.alloc(2432), af32.alloc(2432)]
        vscr = [[af32.alloc(512), af32.alloc(512)], [af32.alloc(512), af32.alloc(512)]]
        rf = af32.alloc(512)

        kvl = C("kvl")
        wqL = [C("wqL0"), C("wqL1")]
        rrb = C("rrb")
        ptbL = [[C("ptbL00"), C("ptbL01"), C("ptbL02")],
                [C("ptbL10"), C("ptbL11"), C("ptbL12")]]

        dma("sync", kT4, kv_dram[:, 0:SEQ], kvl.inc(16), waits=[("kvb", 32)])
        dma("sync", V4, kv_dram[:, SEQ:2 * SEQ], kvl.inc(16))
        dma("sync", m2_s[0], d["m2q0"], kvl.inc(16))
        dma("sync", m2_s[1], d["m2q1"], kvl.inc(16))
        kvw = kvl.w()

        KC_OF_QB = (8, 16)
        ptb_free = [[[], [], []], [[], [], []]]
        vs_free = [[[], []], [[], []]]
        rf_free = []
        rr_free = []
        for hp in range(NH // 2 if KBISECT >= 4 else 0):
            heads = (2 * hp, 2 * hp + 1)
            for s, h in enumerate(heads):
                wb = h % 2
                dma("gpsimd", wqb[wb], d["wq"][h], wqL[wb].inc(16),
                    waits=(banks[0].wdone() + banks[1].wdone()) if h >= 2 else ())
            sc_bk = [[banks[2], banks[6]], [banks[3], banks[7]]]
            av_bk = [banks[4], banks[5]]
            sum_bk = [banks[0], banks[1]]
            q_ready = []
            for s, h in enumerate(heads):
                wb = h % 2
                qhf = (sc_bk[s][0].wdone() + sc_bk[s][1].wdone()) if hp else []
                for half in range(2):
                    bk = banks[half]
                    for hc in range(HC):
                        mmop(bk.ap, wqb[wb][:, hc * DH:(hc + 1) * DH],
                             xn_own[:, hc * TOWN + half * 512: hc * TOWN + half * 512 + 512],
                             hc == 0, hc == HC - 1,
                             waits=(bk.ww() + wqL[wb].w()) if hc == 0 else [],
                             inc=bk.wi() if hc == HC - 1 else None)
                    act(qh[s][:, half * 512:half * 512 + 512], bk.ap, AF.Copy,
                        scale=sd / slopesf[h], waits=bk.rw() + qhf, inc=bk.ri())
                q_ready = banks[0].rdone() + banks[1].rdone()
            for qb in range(2):
                nkc = KC_OF_QB[qb]

                def emit_scores(kc):
                    for s, h in enumerate(heads):
                        vb = kc % 2
                        bk = sc_bk[s][vb]
                        mmop(bk.ap, kT4[:, kc * 128:(kc + 1) * 128],
                             qh[s][:, qb * 512:qb * 512 + 512], True, True,
                             waits=bk.ww() + q_ready + kvw, inc=bk.wi())

                emit_scores(0)
                for kc in range(nkc):
                    if kc + 1 < nkc:
                        emit_scores(kc + 1)
                    for s, h in enumerate(heads):
                        vb = kc % 2
                        tb = kc % 3
                        bk = sc_bk[s][vb]
                        m2v = m2_s[qb][:, 1920 - kc * 128: 1920 - kc * 128 + 512]
                        vop("vector", lambda e, o=vscr[s][vb], i=bk.ap, m=m2v:
                            e.scalar_tensor_tensor(o, i, 1.0, m, ALU.mult, ALU.add),
                            waits=bk.rw() + vs_free[s][vb], inc=bk.ri())
                        act(ptb[s][tb], vscr[s][vb], AF.Exp, scale=slopesf[h],
                            waits=bk.rdone() + ptb_free[s][tb],
                            inc=ptbL[s][tb].inc())
                        vs_free[s][vb] = ptbL[s][tb].w()
                        ptw = ptbL[s][tb].w()
                        mmop(sum_bk[s].ap[0:1, :], onescol, ptb[s][tb],
                             kc == 0, kc == nkc - 1,
                             waits=ptw + (sum_bk[s].ww() if kc == 0 else []),
                             inc=sum_bk[s].wi() if kc == nkc - 1 else None)
                        lastmm = (kc == nkc - 1)
                        mmop(av_bk[s].ap, V4[:, kc * 128:(kc + 1) * 128], ptb[s][tb],
                             kc == 0, lastmm,
                             waits=ptw + (av_bk[s].ww() if kc == 0 else []),
                             inc=av_bk[s].wi() if lastmm else (f"ptbF{s}{tb}", 1))
                        if lastmm:
                            ptb_free[s][tb] = [(f"{av_bk[s].name}W", av_bk[s].w)]
                        else:
                            ptb_free[s][tb] = [(f"ptbF{s}{tb}", p.cur(f"ptbF{s}{tb}"))]
                for s, h in enumerate(heads):
                    vop("vector", lambda e, o=rf[0:1, :], i=sum_bk[s].ap[0:1, :]:
                        e.reciprocal(o, i),
                        waits=sum_bk[s].rw() + rr_free, inc=sum_bk[s].ri())
                    vop("vector", lambda e, o=rrow_b[0:1, :], i=rf[0:1, :]:
                        e.tensor_copy(o, i), inc=rrb.inc())
                    rep_bk = sc_bk[s][0]
                    mmop(rep_bk.ap, ones1, rrow_b[0:1, :], True, True,
                         waits=rep_bk.ww() + rrb.w(), inc=rep_bk.wi())
                    rr_free = [(f"{rep_bk.name}W", rep_bk.w)]
                    act(rf, rep_bk.ap, AF.Copy,
                        waits=rep_bk.rw() + rf_free, inc=rep_bk.ri())
                    vop("vector", lambda e,
                        o=avT[:, h * TOWN + qb * 512: h * TOWN + qb * 512 + 512],
                        a=av_bk[s].ap, b=rf: e.tensor_tensor(o, a, b, ALU.mult),
                        waits=av_bk[s].rw() + rep_bk.rdone(), inc=av_bk[s].ri())
                    rf_free = [(f"{av_bk[s].name}R", av_bk[s].r)]
        av_done = banks[4].rdone() + banks[5].rdone()

        abf.pop()
        af32.pop()
        p.barrier("ph4")

        abf.push()
        af32.push()
        wob = abf.alloc(HC * 512)
        gio = [abf.alloc(512), abf.alloc(512)]
        xio = [af32.alloc(512), af32.alloc(512)]
        oscr = [af32.alloc(512), af32.alloc(512)]
        oout = [af32.alloc(512), af32.alloc(512)]
        woL = C("woL")
        giL = [C("giL0"), C("giL1")]
        xiL = [C("xiL0"), C("xiL1")]
        ooL = [C("ooL0"), C("ooL1")]
        ooF = [C("ooF0"), C("ooF1")]
        g_all = (gsf[0].w() + gsf[1].w()) if full_ffn else []
        gi_free = [[], []]
        os_free = [[], []]

        if KBISECT < 5:
            p.op("gpsimd", lambda e, o=oout[0]: e.memset(o, 0.0), inc=("stub", 1))
            for hg in range(8):
                for ti in range(8):
                    dma("sync", out_d[ti * 128:(ti + 1) * 128, hg * 512:(hg + 1) * 512],
                        oout[0], ("stubo", 16), waits=[("stub", 1)])
            p.op("sync", lambda e: e.nop(), waits=[("stubo", p.cur("stubo"))])
        for hg in range(8 if KBISECT >= 5 else 0):
            dma("gpsimd", wob, d["wo"][hg], woL.inc(16),
                waits=(banks[0].wdone() + banks[1].wdone() +
                       banks[2].wdone() + banks[3].wdone()) if hg else ())
            for ti in range(8):
                it = hg * 8 + ti
                gb2 = it % 2
                dma("sync", gio[gb2], gbounce[ti * 128:(ti + 1) * 128, hg * 512:(hg + 1) * 512],
                    giL[gb2].inc(16), waits=gi_free[gb2] + g_all)
                xb = it % 2
                dma("sync", xio[xb], d["xown"][ti * 128:(ti + 1) * 128, hg * 512:(hg + 1) * 512],
                    xiL[xb].inc(16), waits=ooL[xb].w())
                bk = banks[ti % 4]
                for hh in range(NH):
                    mmop(bk.ap, avT[:, hh * TOWN + ti * 128: hh * TOWN + ti * 128 + 128],
                         wob[:, hh * 512:(hh + 1) * 512], hh == 0, hh == NH - 1,
                         waits=(bk.ww() + woL.w() + av_done) if hh == 0 else [],
                         inc=bk.wi() if hh == NH - 1 else None)
                ob = it % 2
                vop("vector", lambda e, o=oscr[ob], a=bk.ap, b=gio[gb2]:
                    e.tensor_tensor(o, a, b, ALU.mult),
                    waits=bk.rw() + giL[gb2].w() + os_free[ob], inc=bk.ri())
                gi_free[gb2] = bk.rdone()
                vop("vector", lambda e, o=oout[ob], a=oscr[ob], b=xio[xb]:
                    e.tensor_tensor(o, a, b, ALU.add),
                    waits=bk.rdone() + xiL[xb].w() + ooF[ob].w(), inc=ooL[ob].inc())
                os_free[ob] = ooL[ob].w()
                dma("sync", out_d[ti * 128:(ti + 1) * 128, hg * 512:(hg + 1) * 512],
                    oout[ob], ooF[ob].inc(16), waits=ooL[ob].w())
        if KB_DBG:
            dma("sync", dbg_xn, xn_own, ("dbgs", 16))
            dma("sync", dbg_av, avT, ("dbgs", 16))
            p.op("sync", lambda e: e.nop(), waits=[("dbgs", p.cur("dbgs"))])
        if KBISECT >= 5:
            p.op("sync", lambda e: e.nop(), waits=ooF[0].w() + ooF[1].w())

        with contextlib.ExitStack() as ss:
            sems = {n: ss.enter_context(nc.semaphore(n)) for n in p.plan}
            with nc.Block() as blk:
                @blk.tensor
                def _(e):
                    p.emit("tensor", e, sems)

                @blk.vector
                def _(e):
                    p.emit("vector", e, sems)

                @blk.scalar
                def _(e):
                    p.emit("scalar", e, sems)

                @blk.gpsimd
                def _(e):
                    p.emit("gpsimd", e, sems)

                @blk.sync
                def _(e):
                    p.emit("sync", e, sems)
        pstack.close()
    return nc


def kernel(**inputs):
    maps, own_blocks = host_prep(inputs)
    nc = build_program()
    res = run_bass_kernel_spmd(nc, maps, core_ids=list(range(NCORES)))
    out = np.zeros((4, SEQ, H), dtype=np.float32)
    for core in range(NCORES):
        seq = core // 2
        blocks = own_blocks[core % 2]
        o = np.asarray(res.results[core]["out"])
        out[seq, blocks[0][0]:blocks[0][1], :] = o[0:512, :]
        out[seq, blocks[1][0]:blocks[1][1], :] = o[512:1024, :]
    return out
